# version 43
# baseline (speedup 1.0000x reference)
import contextlib
import numpy as np
import ml_dtypes
import concourse.bass as bass
import concourse.mybir as mybir
from concourse.bass_utils import run_bass_kernel_spmd

F32 = mybir.dt.float32
BF16 = mybir.dt.bfloat16
I32 = mybir.dt.int32
AF = mybir.ActivationFunctionType
ALU = mybir.AluOpType
AXL = mybir.AxisListType

D = 1024
T = 2048
NB = 2
NCH = 8
H = 16
EPS = 1e-5
LN1E4 = float(np.log(10000.0))
ENGINES = ("tensor", "vector", "scalar", "gpsimd", "sync")


def ts(i, n=128):
    return slice(i * n, (i + 1) * n)


class Prog:

    def __init__(self):
        self.ops = []
        self.fds = []
        self.last_w = {}
        self.readers = {}
        self.bar = set()
        self.default_fd = 0

    import contextlib as _ctx

    @_ctx.contextmanager
    def fd(self, v):
        old = self.default_fd
        self.default_fd = v
        try:
            yield
        finally:
            self.default_fd = old

    def op(self, engine, fn, reads=(), writes=(), fd=None):
        self.fds.append(self.default_fd if fd is None else fd)
        deps = set(self.bar)
        for r in reads:
            if r in self.last_w:
                deps.add(self.last_w[r])
        for w in writes:
            if w in self.last_w:
                deps.add(self.last_w[w])
            deps.update(self.readers.get(w, ()))
        i = len(self.ops)
        self.ops.append((engine, fn, deps))
        for r in reads:
            self.readers.setdefault(r, []).append(i)
        for w in writes:
            self.last_w[w] = i
            self.readers[w] = []
        return i

    def barrier(self):
        tails = {}
        for i, (en, _, _) in enumerate(self.ops):
            tails[en] = i
        self.bar = set(tails.values())

    def materialize(self, block, sems, dmapool):
        ops = self.ops
        n = len(ops)
        needed = [False] * n
        for en, fn, deps in ops:
            for d in deps:
                if ops[d][0] != en:
                    needed[d] = True
        for i, (en, fn, deps) in enumerate(ops):
            if en == "sync":
                needed[i] = True
        if n:
            needed[n - 1] = True
        val = [0] * n
        slot = [None] * n
        cnt = {e: 0 for e in ENGINES}
        NSLOT = len(dmapool)
        uses = [0] * NSLOT
        dma_i = 0
        for i, (en, fn, deps) in enumerate(ops):
            if en == "sync":
                s = dma_i % NSLOT
                slot[i] = s
                uses[s] += 1
                val[i] = 16 * uses[s]
                dma_i += 1
            else:
                if needed[i]:
                    cnt[en] += 1
                val[i] = cnt[en]
        fds = self.fds

        def section(eng_obj, name):
            hw = {}
            last_drain = -1
            do_drain = name in ("vector", "scalar", "gpsimd")
            my_dma = 0
            for i, (en, fn, deps) in enumerate(ops):
                if en != name:
                    continue
                byen = {}
                need_drain = False
                for d in deps:
                    den = ops[d][0]
                    if den == "sync":
                        k = ("D", slot[d])
                        byen[k] = max(byen.get(k, 0), val[d])
                    elif den != name:
                        byen[den] = max(byen.get(den, 0), val[d])
                    elif d >= last_drain and min(fds[i], fds[d]) < 256:
                        need_drain = True
                if name == "sync":
                    s = slot[i]
                    if val[i] > 16:
                        k = ("D", s)
                        byen[k] = max(byen.get(k, 0), val[i] - 16)
                for k, v in sorted(byen.items(), key=str):
                    if v > hw.get(k, 0):
                        if isinstance(k, tuple):
                            eng_obj.wait_ge(dmapool[k[1]], v)
                        else:
                            eng_obj.wait_ge(sems[k], v)
                        hw[k] = v
                if do_drain and need_drain:
                    eng_obj.drain()
                    last_drain = i
                ins = fn(eng_obj)
                if name == "sync":
                    ins.then_inc(dmapool[slot[i]], 16)
                elif needed[i]:
                    ins.then_inc(sems[name], 1)

        @block.tensor
        def _(e):
            section(e, "tensor")

        @block.vector
        def _(e):
            section(e, "vector")

        @block.scalar
        def _(e):
            section(e, "scalar")

        @block.gpsimd
        def _(e):
            section(e, "gpsimd")

        @block.sync
        def _(e):
            section(e, "sync")


def build_nc(n_sub=4, debug=(), probe=False):
    nc = bass.Bass("TRN2", target_bir_lowering=False, debug=False)
    P = Prog()

    def din(name, shape, dt=F32):
        return nc.dram_tensor(name, list(shape), dt, kind="ExternalInput").ap()

    xT = din("xT", [NB, D, T])
    wq = din("wq", [D, D])
    wk = din("wk", [D, D])
    wv = din("wv", [D, D])
    woa = din("woa", [D, D])
    bqc = din("bqc", [128, 8])
    bkc = din("bkc", [128, 8])
    bvr = din("bvr", [1, D])
    boac = din("boac", [128, 8])
    qT = din("qT", [6, D, D], I32)
    screp = din("screp", [6, D, D])
    laT = din("laT", [6, D, 32])
    lbT = din("lbT", [6, 32, D])
    fbias = din("fbias", [128, 48])
    gamc = din("gamc", [128, 24])
    betc = din("betc", [128, 24])
    wdT = din("wdT", [D, D])
    wobT = din("wobT", [D, D])
    bdc = din("bdc", [128, 8])
    bobc = din("bobc", [128, 8])
    alog = din("alog", [D, 16])
    bpar = din("bpar", [D, 16])

    out = nc.dram_tensor("outT", [NB, D, T], F32, kind="ExternalOutput").ap()
    Xs = [nc.dram_tensor("Xscr%d" % i, [NB, D, T], F32, kind="Internal").ap()
          for i in range(2)]
    wstash = nc.dram_tensor("wstash", [6, D, D], BF16, kind="Internal").ap()
    wbstash = nc.dram_tensor("wbstash", [6, 128, NCH * 1024], BF16, kind="Internal").ap()
    probeH = probeF = None
    if probe:
        probeH = nc.dram_tensor("probeH", [20, 128, T], BF16, kind="ExternalOutput").ap()
        probeF = nc.dram_tensor("probeF", [9, 128, T], F32, kind="ExternalOutput").ap()
    dbg = {}
    for s in debug:
        dbg[s] = nc.dram_tensor("dbgX%d" % s, [NB, D, T], F32,
                                kind="ExternalOutput").ap()

    es = contextlib.ExitStack()
    with es:
        def sb(name, shape, dt):
            return es.enter_context(nc.sbuf_tensor(name, shape, dt))

        sems = {e: es.enter_context(nc.semaphore("sem_" + e)) for e in ENGINES}
        dmapool = [es.enter_context(nc.semaphore("dmap%d" % i)) for i in range(64)]
        block = es.enter_context(nc.Block())

        ones_stat = sb("ones_stat", [128, 128], BF16)
        ones_full = sb("ones_full", [128, 128], BF16)
        bo4 = sb("bo4", [128, 4], BF16)
        biasq = sb("biasq", [128, 8], F32)
        biask = sb("biask", [128, 8], F32)
        biasoa = sb("biasoa", [128, 8], F32)
        nrm = sb("nrm", [128, 8], F32)
        epsc = sb("epsc", [128, 2], F32)
        id4 = sb("id4", [128, 4], BF16)
        dbt = sb("dbt", [128, 8], F32)
        sel4 = sb("sel4", [128, 512], BF16)
        lastore = sb("lastore", [128, 1536], BF16)
        fbc = sb("fbc", [128, 48], F32)
        gmc = sb("gmc", [128, 24], F32)
        btc = sb("btc", [128, 24], F32)
        stat4 = sb("stat4", [128, 16], BF16)
        bdcol = sb("bdcol", [128, 8], F32)
        bobcol = sb("bobcol", [128, 8], F32)

        srep = sb("srep", [128, T], BF16)
        msrep = sb("msrep", [128, T], BF16)
        sqb = sb("sqb", [128, T], BF16)
        statf = sb("statf", [128, T + 512], F32)
        fb = sb("fb", [128, 2 * T], F32)
        scr = sb("scr", [128, T], F32)
        xhat = sb("xhat", [128, NCH * T], BF16)

        ps = [es.enter_context(nc.psum_tensor("ps%d" % i, [128, 512], F32))
              for i in range(8)]

        def dma(dst, src, reads=(), writes=()):
            return P.op("sync", lambda e: e.dma_start(out=dst, in_=src),
                        reads=reads, writes=writes)

        def dump(which, idx, ap, reads, w=None, p=128):
            if not probe:
                return
            tgt = probeH if which == "H" else probeF
            dst = tgt[idx]
            if w is not None or p != 128:
                dst = dst[0:p, 0:(w if w is not None else T)]
            dma(dst, ap, reads=list(reads), writes=["PRB"])

        PRE = ["pre"]

        def pre(eng, fn):
            P.op(eng, fn, reads=PRE, writes=PRE)

        pre("vector", lambda e: e.memset(ones_stat[:], 1.0 / 1024.0))
        pre("vector", lambda e: e.memset(ones_full[:], 1.0))
        pre("vector", lambda e: e.memset(epsc[:, 0:1], EPS))
        pre("vector", lambda e: e.memset(epsc[:, 1:2], 1e-20))
        pre("vector", lambda e: e.memset(bo4[:], 0.0))
        for h4 in range(4):
            pre("vector", lambda e, h4=h4: e.memset(bo4[ts(h4, 32), h4:h4 + 1], 1.0))

        pre("gpsimd", lambda e: e.iota(statf[:, T:T + 4], [[1, 4]], base=0,
                                       channel_multiplier=-1,
                                       allow_small_or_imprecise_dtypes=True))
        pre("vector", lambda e: e.tensor_scalar(id4[:], statf[:, T:T + 4],
                                                0.0, None, ALU.is_equal))
        pre("gpsimd", lambda e: e.iota(scr[:, 512:1024], [[1, 4], [0, 128]], base=0,
                                       channel_multiplier=-1,
                                       allow_small_or_imprecise_dtypes=True))
        pre("vector", lambda e: e.tensor_scalar(sel4[:], scr[:, 512:1024], 0.0,
                                                None, ALU.is_equal))
        pre("vector", lambda e: e.memset(stat4[:], 0.0))
        for j in range(4):
            pre("vector", lambda e, j=j: e.memset(stat4[:, 4 * j + j: 4 * j + j + 1],
                                                  1.0 / 1024.0))
        dma(fbc[:], fbias, reads=PRE, writes=PRE)
        dma(gmc[:], gamc, reads=PRE, writes=PRE)
        dma(btc[:], betc, reads=PRE, writes=PRE)
        dma(bdcol[:], bdc, reads=PRE, writes=PRE)
        dma(bobcol[:], bobc, reads=PRE, writes=PRE)
        for l in range(6):
            for c in range(NCH):
                dma(scr[:, 0:32], laT[l, ts(c), :], reads=PRE, writes=PRE)
                pre("vector", lambda e, l=l, c=c: e.tensor_copy(
                    lastore[:, l * 256 + c * 32: l * 256 + (c + 1) * 32],
                    scr[:, 0:32]))
        P.default_fd = 512
        for l in range(6):
            for c in range(NCH):
                p_ = c % 2
                o_ = p_ * 1024
                dma(fb[:, o_:o_ + 1024], screp[l, ts(c), :], writes=["dqa%d" % p_])
                dma(statf[:, o_:o_ + 1024].bitcast(I32), qT[l, ts(c), :],
                    writes=["dqb%d" % p_])
                P.op("vector", lambda e, o_=o_: e.tensor_scalar(
                    scr[:, o_:o_ + 1024], statf[:, o_:o_ + 1024].bitcast(I32),
                    1.0 / 7.5, -1.0, ALU.mult, ALU.add),
                    reads=["dqb%d" % p_], writes=["dqc%d" % p_, "dqb%d" % p_])
                P.op("vector", lambda e, o_=o_: e.tensor_tensor(
                    sqb[:, o_:o_ + 1024], scr[:, o_:o_ + 1024], fb[:, o_:o_ + 1024],
                    ALU.mult),
                    reads=["dqc%d" % p_, "dqa%d" % p_],
                    writes=["dqd%d" % p_, "dqa%d" % p_, "dqc%d" % p_])
                dma(wstash[l, ts(c), :], sqb[:, o_:o_ + 1024],
                    reads=["dqd%d" % p_], writes=["dqd%d" % p_])
        for wi, wap in enumerate([wq, wk, wv, woa, wdT, wobT]):
            for c in range(NCH):
                p_ = c % 2
                o_ = p_ * 1024
                dma(fb[:, o_:o_ + 1024], wap[ts(c), :], writes=["dqa%d" % p_])
                P.op("vector", lambda e, o_=o_: e.tensor_copy(
                    sqb[:, o_:o_ + 1024], fb[:, o_:o_ + 1024]),
                    reads=["dqa%d" % p_], writes=["dqd%d" % p_, "dqa%d" % p_])
                dma(wbstash[wi, :, c * 1024:(c + 1) * 1024], sqb[:, o_:o_ + 1024],
                    reads=["dqd%d" % p_], writes=["dqd%d" % p_])
        P.op("vector", lambda e: e.engine_nop(),
             reads=["dqa0", "dqa1", "dqb0", "dqb1", "dqd0", "dqd1"], writes=PRE)
        P.default_fd = 0
        pre("gpsimd", lambda e: e.iota(scr[:, 0:1], [[1, 1]], base=0,
                                       channel_multiplier=1,
                                       allow_small_or_imprecise_dtypes=True))
        pre("vector", lambda e: e.tensor_scalar(scr[:, 4:5], scr[:, 0:1], 32.0, None, ALU.is_ge))
        pre("vector", lambda e: e.tensor_scalar(scr[:, 5:6], scr[:, 0:1], 64.0, None, ALU.is_ge))
        pre("vector", lambda e: e.tensor_scalar(scr[:, 6:7], scr[:, 0:1], 96.0, None, ALU.is_ge))
        pre("vector", lambda e: e.tensor_tensor(scr[:, 4:5], scr[:, 4:5], scr[:, 5:6], ALU.add))
        pre("vector", lambda e: e.tensor_tensor(scr[:, 4:5], scr[:, 4:5], scr[:, 6:7], ALU.add))
        pre("vector", lambda e: e.scalar_tensor_tensor(scr[:, 1:2], scr[:, 4:5], -32.0,
                                                       scr[:, 0:1], ALU.mult, ALU.add))
        pre("scalar", lambda e: e.activation(scr[:, 2:3], scr[:, 1:2], AF.Exp,
                                             scale=-LN1E4 / 32.0))
        fcol, f2 = scr[:, 2:3], scr[:, 3:4]
        C = fb
        S = statf
        pre("vector", lambda e: e.tensor_tensor(f2, fcol, fcol, ALU.mult))
        pre("vector", lambda e: e.tensor_scalar(scr[:, 4:5], f2, -1 / 30., 1., ALU.mult, ALU.add))
        pre("vector", lambda e: e.tensor_tensor(scr[:, 5:6], scr[:, 4:5], f2, ALU.mult))
        pre("vector", lambda e: e.tensor_scalar(scr[:, 5:6], scr[:, 5:6], -1 / 12., 1., ALU.mult, ALU.add))
        pre("vector", lambda e: e.tensor_tensor(scr[:, 6:7], scr[:, 5:6], f2, ALU.mult))
        pre("vector", lambda e: e.tensor_scalar(C[:, 1:2], scr[:, 6:7], -0.5, 1., ALU.mult, ALU.add))
        pre("vector", lambda e: e.tensor_scalar(scr[:, 4:5], f2, -1 / 42., 1., ALU.mult, ALU.add))
        pre("vector", lambda e: e.tensor_tensor(scr[:, 5:6], scr[:, 4:5], f2, ALU.mult))
        pre("vector", lambda e: e.tensor_scalar(scr[:, 5:6], scr[:, 5:6], -1 / 20., 1., ALU.mult, ALU.add))
        pre("vector", lambda e: e.tensor_tensor(scr[:, 6:7], scr[:, 5:6], f2, ALU.mult))
        pre("vector", lambda e: e.tensor_scalar(scr[:, 6:7], scr[:, 6:7], -1 / 6., 1., ALU.mult, ALU.add))
        pre("vector", lambda e: e.tensor_tensor(S[:, 1:2], scr[:, 6:7], fcol, ALU.mult))
        pre("vector", lambda e: e.memset(C[:, 0:1], 1.0))
        pre("vector", lambda e: e.memset(S[:, 0:1], 0.0))
        if probe:
            pre("vector", lambda e: e.tensor_copy(dbt[:, 4:5], fcol))
            pre("vector", lambda e: e.tensor_copy(dbt[:, 5:6], f2))
            pre("vector", lambda e: e.tensor_copy(dbt[:, 6:7], C[:, 1:2]))
            pre("vector", lambda e: e.tensor_copy(dbt[:, 7:8], S[:, 1:2]))
        known = 2
        while known < T:
            s = known - 1
            m = min(known - 1, T - known)
            pre("vector", lambda e, s=s, m=m, k=known: e.tensor_scalar(
                fb[:, T:T + m], C[:, 1:1 + m], C[:, s:s + 1], None, ALU.mult))
            pre("vector", lambda e, s=s, m=m, k=known: e.tensor_scalar(
                scr[:, 0:m], S[:, 1:1 + m], S[:, s:s + 1], None, ALU.mult))
            pre("vector", lambda e, s=s, m=m, k=known: e.tensor_scalar(
                scr[:, 1024:1024 + m], S[:, 1:1 + m], C[:, s:s + 1], None, ALU.mult))
            pre("vector", lambda e, s=s, m=m, k=known: e.tensor_scalar(
                fb[:, T + 1024:T + 1024 + m], C[:, 1:1 + m], S[:, s:s + 1], None, ALU.mult))
            pre("vector", lambda e, s=s, m=m, k=known: e.tensor_tensor(
                C[:, k:k + m], fb[:, T:T + m], scr[:, 0:m], ALU.subtract))
            pre("vector", lambda e, s=s, m=m, k=known: e.tensor_tensor(
                S[:, k:k + m], scr[:, 1024:1024 + m], fb[:, T + 1024:T + 1024 + m], ALU.add))
            known += m
        dma(biasq[:], bqc, reads=PRE, writes=PRE)
        dma(biask[:], bkc, reads=PRE, writes=PRE)
        dma(biasoa[:], boac, reads=PRE, writes=PRE)
        P.barrier()

        def ln_phase(xin, b):
            for c in range(NCH):
                fq = ["fb:%d" % q for q in range((c % 2) * 4, (c % 2) * 4 + 4)]
                dma(fb[:, ts(c % 2, T)], xin[b, ts(c), :], writes=fq)
                P.op("scalar",
                     lambda e, c=c: e.copy(xhat[:, ts(c, T)], fb[:, ts(c % 2, T)]),
                     reads=fq, writes=["xh%d" % c])
                P.op("scalar",
                     lambda e, c=c: e.activation(sqb[:], fb[:, ts(c % 2, T)], AF.Square),
                     reads=fq, writes=["sqb"])
                for j in range(4):
                    P.op("tensor",
                         lambda e, c=c, j=j: e.matmul(
                             ps[j][:], ones_stat[:],
                             xhat[:, c * T + 512 * j: c * T + 512 * (j + 1)],
                             start=(c == 0), stop=(c == NCH - 1)),
                         reads=["xh%d" % c, "PS%d" % j], writes=["PS%d" % j])
                    P.op("tensor",
                         lambda e, c=c, j=j: e.matmul(
                             ps[4 + j][:], ones_stat[:],
                             sqb[:, 512 * j: 512 * (j + 1)],
                             start=(c == 0), stop=(c == NCH - 1)),
                         reads=["sqb", "PS%d" % (4 + j)], writes=["PS%d" % (4 + j)])
            for j in range(4):
                sl = slice(512 * j, 512 * (j + 1))
                tmp = statf[:, T:T + 512]
                P.op("scalar", lambda e, j=j, sl=sl: e.copy(statf[:, sl], ps[j][:]),
                     reads=["PS%d" % j], writes=["statf", "PS%d" % j])
                P.op("vector", lambda e, sl=sl, tmp=tmp: e.tensor_tensor(
                    tmp, statf[:, sl], statf[:, sl], ALU.mult),
                    reads=["statf"], writes=["statf"])
                P.op("vector", lambda e, j=j, tmp=tmp: e.tensor_tensor(
                    tmp, ps[4 + j][:], tmp, ALU.subtract),
                    reads=["statf", "PS%d" % (4 + j)],
                    writes=["statf", "PS%d" % (4 + j)])
                P.op("scalar", lambda e, tmp=tmp: e.activation(tmp, tmp, AF.Ln, bias=epsc[:, 0:1]),
                     reads=["statf"], writes=["statf"])
                P.op("scalar", lambda e, sl=sl, tmp=tmp: e.activation(
                    srep[:, sl], tmp, AF.Exp, scale=-0.5),
                    reads=["statf"], writes=["srep"])
                P.op("vector", lambda e, sl=sl: e.tensor_tensor(
                    msrep[:, sl], statf[:, sl], srep[:, sl], ALU.mult),
                    reads=["statf", "srep"], writes=["msrep"])
            for c in range(NCH):
                P.op("vector", lambda e, c=c: e.tensor_tensor(
                    xhat[:, ts(c, T)], xhat[:, ts(c, T)], srep[:], ALU.mult),
                    reads=["xh%d" % c, "srep"], writes=["xh%d" % c])
                P.op("vector", lambda e, c=c: e.tensor_tensor(
                    xhat[:, ts(c, T)], xhat[:, ts(c, T)], msrep[:], ALU.subtract),
                    reads=["xh%d" % c, "msrep"], writes=["xh%d" % c])

        def attn_a(xin, xout):
            aes = contextlib.ExitStack()
            with aes:
                def asb(name, shape, dt):
                    return aes.enter_context(nc.sbuf_tensor(name, shape, dt))
                vstore = asb("vstore", [128, 16 * 1040], BF16)
                qk = asb("qk", [128, 4 * T], BF16)
                ptil = asb("ptil", [128, 4 * 512], BF16)
                otq = asb("otq", [128, NCH * T], BF16)
                osc = asb("osc", [128, 512], BF16)
                wvb = asb("wvb", [128, 4096], BF16)
                wqk = asb("wqk", [128, 4 * NCH * 128], BF16)
                cos_t = asb("cos_t", [128, T], BF16)
                sin_t = asb("sin_t", [128, T], BF16)
                masks = asb("masks", [128, 896], BF16)
                bvrow = asb("bvrow", [1, D], BF16)
                bct = asb("bct", [128, 8], BF16)
                selH = asb("selH", [128, 1024], BF16)
                denst = asb("denst", [128, T], BF16)
                rrb = asb("rrb", [128, T], BF16)

                P.op("vector", lambda e: e.tensor_copy(cos_t[:], fb[:, 0:T]),
                     reads=PRE + ["fb:0", "fb:1", "fb:2", "fb:3"], writes=["cos_t"])
                P.op("vector", lambda e: e.tensor_copy(sin_t[:], statf[:, 0:T]),
                     reads=PRE + ["statf"], writes=["sin_t"])
                P.op("gpsimd", lambda e: e.iota(scr[:, 0:896], [[1, 896]], base=-384,
                                                channel_multiplier=-1,
                                                allow_small_or_imprecise_dtypes=True),
                     reads=PRE + ["scr"], writes=["scr"])
                P.op("vector", lambda e: e.tensor_scalar(
                    masks[:], scr[:, 0:896], 0.0, None, ALU.is_ge),
                    reads=["scr"], writes=["masks"])
                dma(scr[0:1, 0:D], bvr, reads=PRE, writes=["scr"])
                P.op("vector", lambda e: e.tensor_copy(bvrow[:], scr[0:1, 0:D]),
                     reads=["scr"], writes=["bvrow"])
                P.op("gpsimd", lambda e: e.iota(scr[:, 0:1024],
                                                [[2, 8], [1, 2], [0, 64]], base=0,
                                                channel_multiplier=-1,
                                                allow_small_or_imprecise_dtypes=True),
                     reads=["scr"], writes=["scr"])
                P.op("vector", lambda e: e.tensor_scalar(
                    selH[:], scr[:, 0:1024], 0.0, None, ALU.is_equal),
                    reads=["scr"], writes=["selH"])

                def load_w(tag, wi, dst, c0, width, extra=()):
                    srcs = wbstash[wi].rearrange("p (c f) -> p c f", c=NCH)[
                        :, :, c0:c0 + width]
                    dsts = dst[:, 0:NCH * width].rearrange("p (c f) -> p c f", c=NCH)
                    dma(dsts, srcs, reads=list(extra), writes=[tag])

                for b in range(NB):
                    ln_phase(xin, b)
                    if b == 0:
                        dump("H", 0, xhat[:, 0:T], ["xh0"])
                        dump("H", 6, srep[:], ["srep"])
                        dump("H", 7, msrep[:], ["msrep"])
                        dump("F", 0, statf[:, 0:T], ["statf"])
                    P.op("vector", lambda e: e.memset(
                        vstore.ap().rearrange("p (t g c) -> p (t g) c",
                                              t=16, g=16, c=65)[:, :, 64:65], 1.0),
                        writes=["vones"])
                    for half in range(2):
                        load_w("wvbA", 2, wvb, half * 512, 512)
                        for tt in range(16):
                            pid = tt % 4
                            for c in range(NCH):
                                P.op("tensor", lambda e, tt=tt, c=c, pid=pid: e.matmul(
                                    ps[pid][:],
                                    xhat[:, c * T + tt * 128: c * T + (tt + 1) * 128],
                                    wvb[:, c * 512:(c + 1) * 512],
                                    start=(c == 0), stop=False),
                                    reads=["xh%d" % c, "wvbA", "PS%d" % pid],
                                    writes=["PS%d" % pid])
                            P.op("tensor", lambda e, half=half, pid=pid: e.matmul(
                                ps[pid][:], ones_full[0:1, 0:128],
                                bvrow[0:1, half * 512: half * 512 + 512],
                                start=False, stop=True),
                                reads=["PS%d" % pid, "bvrow"], writes=["PS%d" % pid])
                            P.op("scalar", lambda e, tt=tt, half=half, pid=pid: e.copy(
                                vstore.ap()[:, tt * 1040:(tt + 1) * 1040]
                                .rearrange("p (g c) -> p g c", g=16, c=65)
                                [:, half * 8:(half + 1) * 8, 0:64],
                                ps[pid][:]),
                                reads=["PS%d" % pid, "vones"],
                                writes=["vst%d" % tt, "PS%d" % pid])
                    if b == 0:
                        dump("H", 3, vstore.ap()[:, 0:T], ["vst0", "vst1"])
                        dump("H", 8, cos_t[:], ["cos_t"])
                        dump("H", 9, sin_t[:], ["sin_t"])
                    for cp in range(4):
                        for grp in range(2):
                            for wl, ot in enumerate([cp, 4 + cp]):
                                wr = grp * 2 + wl
                                load_w("wqkR%d" % wr, grp,
                                       wqk.ap()[:, wr * NCH * 128:(wr + 1) * NCH * 128],
                                       ot * 128, 128)
                            for wl in range(2):
                                wi = grp * 2 + wl
                                bias = biasq if grp == 0 else biask
                                bcol = cp if wl == 0 else 4 + cp
                                for j in range(4):
                                    pid = j
                                    for c in range(NCH):
                                        P.op("tensor", lambda e, wi=wi, j=j, c=c, pid=pid: e.matmul(
                                            ps[pid][:],
                                            wqk[:, (wi * NCH + c) * 128:(wi * NCH + c + 1) * 128],
                                            xhat[:, c * T + j * 512: c * T + (j + 1) * 512],
                                            start=(c == 0), stop=(c == NCH - 1)),
                                            reads=["wqkR%d" % wi, "xh%d" % c, "PS%d" % pid],
                                            writes=["PS%d" % pid])
                                    P.op("scalar", lambda e, wi=wi, j=j, pid=pid,
                                         bias=bias, bcol=bcol: e.activation(
                                             qk[:, wi * T + j * 512: wi * T + (j + 1) * 512],
                                             ps[pid][:], AF.Identity,
                                             bias=bias[:, bcol:bcol + 1]),
                                         reads=["PS%d" % pid], writes=["qk%d" % wi, "PS%d" % pid])
                        for base in (0, 2):
                            t1, t2 = "qk%d" % base, "qk%d" % (base + 1)
                            a1 = qk[:, base * T:(base + 1) * T]
                            a2 = qk[:, (base + 1) * T:(base + 2) * T]
                            P.op("vector", lambda e, a1=a1: e.tensor_tensor(
                                rrb[:, 0:T], a1, sin_t[:], ALU.mult),
                                reads=[t1], writes=["rscA"])
                            P.op("vector", lambda e, a2=a2: e.tensor_tensor(
                                sqb[:, 0:T], a2, sin_t[:], ALU.mult),
                                reads=[t2], writes=["sqb"])
                            P.op("vector", lambda e, a1=a1: e.tensor_tensor(
                                a1, a1, cos_t[:], ALU.mult), reads=[t1], writes=[t1])
                            P.op("vector", lambda e, a1=a1: e.tensor_tensor(
                                a1, a1, sqb[:, 0:T], ALU.subtract),
                                reads=[t1, "sqb"], writes=[t1])
                            P.op("vector", lambda e, a2=a2: e.tensor_tensor(
                                a2, a2, cos_t[:], ALU.mult), reads=[t2], writes=[t2])
                            P.op("vector", lambda e, a2=a2: e.tensor_tensor(
                                a2, a2, rrb[:, 0:T], ALU.add),
                                reads=[t2, "rscA"], writes=[t2])
                        if b == 0 and cp == 0:
                            dump("H", 1, qk[:, 0:T], ["qk0"])
                            dump("H", 2, qk[:, 2 * T:3 * T], ["qk2"])
                        P.default_fd = 1
                        for which, (b0, b1) in enumerate([(0, 1), (2, 3)]):
                            for j in range(4):
                                for wi in (b0, b1):
                                    P.op("scalar", lambda e, wi=wi, j=j: e.activation(
                                        sqb[:, 0:512],
                                        qk[:, wi * T + j * 512: wi * T + (j + 1) * 512],
                                        AF.Square),
                                        reads=["qk%d" % wi], writes=["sqb"], fd=512)
                                    P.op("tensor", lambda e, wi=wi, b0=b0, b1=b1: e.matmul(
                                        ps[4][0:4, :], bo4[:], sqb[:, 0:512],
                                        start=(wi == b0), stop=(wi == b1)),
                                        reads=["sqb", "PS4"], writes=["PS4"])
                                P.op("vector", lambda e, j=j: e.reduce_max(
                                    scr[0:4, j:j + 1], ps[4][0:4, :], axis=AXL.X),
                                    reads=["PS4"], writes=["scr", "PS4"])
                            P.op("vector", lambda e, which=which: e.reduce_max(
                                scr[0:4, 4 + which:5 + which], scr[0:4, 0:4], axis=AXL.X),
                                reads=["scr"], writes=["scr"])
                        P.op("vector", lambda e: e.tensor_tensor(
                            scr[0:4, 6:7], scr[0:4, 4:5], scr[0:4, 5:6], ALU.mult),
                            reads=["scr"], writes=["scr"])
                        if probe and cp == 0 and b == 0:
                            P.op("vector", lambda e: e.tensor_copy(dbt[0:4, 0:1], scr[0:4, 6:7]), reads=["scr"], writes=["dbt"])
                        P.op("scalar", lambda e: e.activation(
                            scr[0:4, 6:7], scr[0:4, 6:7], AF.Ln, bias=epsc[0:4, 1:2]),
                            reads=["scr"], writes=["scr"])
                        if probe and cp == 0 and b == 0:
                            P.op("vector", lambda e: e.tensor_copy(dbt[0:4, 1:2], scr[0:4, 6:7]), reads=["scr"], writes=["dbt"])
                        P.op("scalar", lambda e: e.activation(
                            scr[0:4, 6:7], scr[0:4, 6:7], AF.Exp, scale=0.5),
                            reads=["scr"], writes=["scr"])
                        if probe and cp == 0 and b == 0:
                            P.op("vector", lambda e: e.tensor_copy(dbt[0:4, 2:3], scr[0:4, 6:7]), reads=["scr"], writes=["dbt"])
                        P.op("vector", lambda e: e.tensor_scalar(
                            scr[0:4, 6:7], scr[0:4, 6:7], -0.125, None, ALU.mult),
                            reads=["scr"], writes=["scr"])
                        if probe and cp == 0 and b == 0:
                            P.op("vector", lambda e: e.tensor_copy(dbt[0:4, 3:4], scr[0:4, 6:7]), reads=["scr"], writes=["dbt"])
                        for _w in range(8):
                            P.op("tensor", lambda e: e.matmul(
                                ps[5][0:1, 4:512], ones_full[0:1, 0:1],
                                masks[0:1, 0:508], start=True, stop=True),
                                reads=[], writes=["WARM"])
                        P.op("vector", lambda e: e.tensor_scalar(
                            bct[0:4, 0:4], id4[0:4, 0:4], scr[0:4, 6:7],
                            None, ALU.mult),
                            reads=["scr"], writes=["bct"])
                        P.op("tensor", lambda e: e.matmul(
                            ps[5][:, 0:4], ones_full[0:4, 0:128], bct[0:4, 0:4],
                            start=True, stop=True),
                            reads=["bct", "PS5"], writes=["PS5"])
                        P.op("scalar", lambda e: e.copy(nrm[:, 0:4], ps[5][:, 0:4]),
                             reads=["PS5"], writes=["nrm0", "nrm1", "nrm2", "nrm3", "PS5"])
                        P.default_fd = 512
                        for h4 in range(4):
                            h = cp * 4 + h4
                            rs = ts(h4, 32)
                            opid = 6 + (h4 % 2)
                            for j in range(4):
                                nkt = 4 * (j + 1)
                                for kt in range(nkt + 1):
                                    if kt < nkt:
                                        pp = kt % 4
                                        P.op("tensor", lambda e, kt=kt, j=j, rs=rs, pp=pp: e.matmul(
                                            ps[pp][:],
                                            qk[rs, 2 * T + kt * 128: 2 * T + (kt + 1) * 128],
                                            qk[rs, j * 512:(j + 1) * 512],
                                            start=True, stop=False,
                                            tile_position=(rs.start, 0)),
                                            reads=["qk0", "qk2", "PS%d" % pp],
                                            writes=["PS%d" % pp])
                                        P.op("tensor", lambda e, kt=kt, j=j, rs=rs, pp=pp: e.matmul(
                                            ps[pp][:],
                                            qk[rs, 3 * T + kt * 128: 3 * T + (kt + 1) * 128],
                                            qk[rs, T + j * 512: T + (j + 1) * 512],
                                            start=False, stop=True,
                                            tile_position=(rs.start, 0)),
                                            reads=["qk1", "qk3", "PS%d" % pp],
                                            writes=["PS%d" % pp])
                                        P.op("scalar", lambda e, kt=kt, h4=h4, pp=pp: e.activation(
                                            ptil[:, (kt % 4) * 512:(kt % 4 + 1) * 512],
                                            ps[pp][:], AF.Exp, scale=0.125,
                                            bias=nrm[:, h4:h4 + 1]),
                                            reads=["PS%d" % pp, "nrm%d" % h4],
                                            writes=["pt%d" % (kt % 4), "PS%d" % pp])
                                        off = kt - 4 * j
                                        if off >= 0:
                                            P.op("vector", lambda e, kt=kt, off=off: e.tensor_tensor(
                                                ptil[:, (kt % 4) * 512:(kt % 4 + 1) * 512],
                                                ptil[:, (kt % 4) * 512:(kt % 4 + 1) * 512],
                                                masks[:, 384 - 128 * off: 896 - 128 * off], ALU.mult),
                                                reads=["pt%d" % (kt % 4)],
                                                writes=["pt%d" % (kt % 4)])
                                    if kt > 0:
                                        ktp = kt - 1
                                        P.op("tensor", lambda e, ktp=ktp, h=h, opid=opid, nkt=nkt: e.matmul(
                                            ps[opid][0:65, :],
                                            vstore[:, ktp * 1040 + h * 65: ktp * 1040 + (h + 1) * 65],
                                            ptil[:, (ktp % 4) * 512:(ktp % 4 + 1) * 512],
                                            start=(ktp == 0), stop=(ktp == nkt - 1)),
                                            reads=["pt%d" % (ktp % 4), "vst%d" % ktp,
                                                   "PS%d" % opid],
                                            writes=["PS%d" % opid])
                                jp = j % 2
                                oscs = osc[:, 0:512]
                                P.op("scalar", lambda e, opid=opid: e.copy(
                                    denst[64:65, 0:512], ps[opid][64:65, :]),
                                    reads=["PS%d" % opid], writes=["dentmp"])
                                dma(denst[h:h + 1, j * 512:(j + 1) * 512],
                                    denst[64:65, 0:512],
                                    reads=["dentmp"], writes=["den%d" % h])
                                ch = h // 2
                                if h % 2 == 0:
                                    P.op("scalar", lambda e, opid=opid, ch=ch, j=j: e.copy(
                                        otq[0:64, ch * T + j * 512: ch * T + (j + 1) * 512],
                                        ps[opid][0:64, :]),
                                        reads=["PS%d" % opid],
                                        writes=["ot%d" % ch, "PS%d" % opid])
                                else:
                                    P.op("scalar", lambda e, opid=opid, oscs=oscs: e.copy(
                                        oscs[0:64, :], ps[opid][0:64, :]),
                                        reads=["PS%d" % opid],
                                        writes=["oscd%d" % jp, "PS%d" % opid])
                                    dma(otq[64:128, ch * T + j * 512: ch * T + (j + 1) * 512],
                                        oscs[0:64, :], reads=["oscd%d" % jp],
                                        writes=["ot%d" % ch])
                    dtags = ["den%d" % hh for hh in range(16)]
                    for _w in range(8):
                        P.op("tensor", lambda e: e.matmul(
                            ps[5][0:1, 4:512], ones_full[0:1, 0:1],
                            masks[0:1, 0:508], start=True, stop=True),
                            reads=[], writes=["WARM"])
                    P.op("vector", lambda e: e.reciprocal(denst[0:16, :], denst[0:16, :]),
                         reads=dtags, writes=["recd"])
                    for ch in range(NCH):
                        for qt in range(4):
                            P.op("tensor", lambda e, ch=ch, qt=qt: e.matmul(
                                ps[qt][:], selH[0:16, ch * 128:(ch + 1) * 128],
                                denst[0:16, qt * 512:(qt + 1) * 512],
                                start=True, stop=True),
                                reads=["recd", "selH", "PS%d" % qt], writes=["PS%d" % qt])
                            P.op("vector", lambda e, ch=ch, qt=qt: e.tensor_copy(
                                rrb[:, qt * 512:(qt + 1) * 512], ps[qt][:]),
                                reads=["PS%d" % qt], writes=["rrb", "PS%d" % qt])
                        P.op("vector", lambda e, ch=ch: e.tensor_tensor(
                            otq[:, ch * T:(ch + 1) * T], otq[:, ch * T:(ch + 1) * T],
                            rrb[:], ALU.mult),
                            reads=["ot%d" % ch, "rrb"], writes=["ot%d" % ch])
                    if b == 0:
                        for cdump in range(NCH):
                            dump("H", 12 + cdump, otq.ap()[:, cdump * T:(cdump + 1) * T],
                                 ["ot%d" % cdump])
                    load_w("wvbA", 3, wvb.ap()[:, 0:1024], 0, 128,
                           extra=["wvbA"])
                    for et in range(NCH):
                        if et + 1 < NCH:
                            par = (et + 1) % 2
                            load_w("wvb" + "AB"[par], 3,
                                   wvb.ap()[:, par * 1024:(par + 1) * 1024],
                                   (et + 1) * 128, 128,
                                   extra=(["wvbA"] if et + 1 == 1 else ()))
                        ho = (et % 2) * 1024
                        for j in range(4):
                            pid = 4 + j
                            for c in range(NCH):
                                P.op("tensor", lambda e, et=et, j=j, c=c, pid=pid, ho=ho: e.matmul(
                                    ps[pid][:],
                                    wvb[:, ho + c * 128: ho + (c + 1) * 128],
                                    otq[:, c * T + j * 512: c * T + (j + 1) * 512],
                                    start=(c == 0), stop=(c == NCH - 1)),
                                    reads=["wvb" + "AB"[et % 2], "ot%d" % c, "PS%d" % pid],
                                    writes=["PS%d" % pid])
                            fq_in = "fb:%d" % (4 + j)
                            fq_out = "fb:%d" % j
                            dma(fb[:, T + j * 512: T + (j + 1) * 512],
                                xin[b, ts(et), j * 512:(j + 1) * 512], writes=[fq_in])
                            P.op("vector", lambda e, j=j, et=et, pid=pid: e.scalar_tensor_tensor(
                                fb[:, j * 512:(j + 1) * 512], ps[pid][:],
                                biasoa[:, et:et + 1],
                                fb[:, T + j * 512: T + (j + 1) * 512],
                                ALU.add, ALU.add),
                                reads=["PS%d" % pid, fq_in], writes=[fq_out, "PS%d" % pid])
                            if probe and b == 0 and et == 0 and j == 0:
                                dump("F", 4, fb[:, 0:512], [fq_out], w=512)
                                dump("F", 5, fb[:, T:T + 512], [fq_in], w=512)
                            dma(xout[b, ts(et), j * 512:(j + 1) * 512],
                                fb[:, j * 512:(j + 1) * 512],
                                reads=[fq_out], writes=["XOUT"])
                    P.barrier()
                P.default_fd = 0
            P.barrier()

        ffn_uid = [0]

        def ffn(xin, xout):
            fes = contextlib.ExitStack()
            uid = ffn_uid[0]
            ffn_uid[0] += 1
            with fes:
                hbuf = fes.enter_context(nc.sbuf_tensor("hbuf%d" % uid, [128, NCH * T], F32))
                yb = fes.enter_context(nc.sbuf_tensor("yb%d" % uid, [128, NCH * T], BF16))
                wstr = fes.enter_context(nc.sbuf_tensor("wstr%d" % uid, [128, 4 * 1024], BF16))
                lbcur = fes.enter_context(nc.sbuf_tensor("lbcur%d" % uid, [32, 1024], BF16))
                stfb = fes.enter_context(nc.sbuf_tensor("stfb%d" % uid, [4, 1024], BF16))
                murep = fes.enter_context(nc.sbuf_tensor("murep%d" % uid, [128, 512], BF16))
                rsrep = fes.enter_context(nc.sbuf_tensor("rsrep%d" % uid, [128, 512], BF16))
                ubuf = fes.enter_context(nc.sbuf_tensor("ubuf%d" % uid, [32, T], BF16))
                stf = scr
                P.default_fd = 512

                def linear(idx, blk, ln_after):
                    wsrc = wstash[idx].rearrange("(c p) f -> p c f", p=128)
                    dma(statf[0:32, 0:1024], lbT[idx], writes=["statf"])
                    P.op("vector", lambda e: e.tensor_copy(lbcur[:], statf[0:32, 0:1024]),
                         reads=["statf"], writes=["FLB"])
                    for qt in range(4):
                        pid = 6 + qt % 2
                        for c in range(NCH):
                            P.op("tensor", lambda e, qt=qt, c=c, pid=pid, idx=idx: e.matmul(
                                ps[pid][0:32, :],
                                lastore[:, idx * 256 + c * 32: idx * 256 + (c + 1) * 32],
                                xhat[:, c * T + qt * 512: c * T + (qt + 1) * 512],
                                start=(c == 0), stop=(c == NCH - 1)),
                                reads=["xh%d" % c, "PS%d" % pid], writes=["PS%d" % pid])
                        P.op("scalar", lambda e, qt=qt, pid=pid: e.copy(
                            ubuf[0:32, qt * 512:(qt + 1) * 512], ps[pid][0:32, :]),
                            reads=["PS%d" % pid], writes=["FU", "PS%d" % pid])
                    for ot in range(NCH):
                        par = ot % 4
                        dma(wstr.ap()[:, par * 1024:(par + 1) * 1024]
                            .rearrange("p (c f) -> p c f", c=NCH),
                            wsrc[:, :, ot * 128:(ot + 1) * 128],
                            writes=["FW%d" % par])
                        for c in range(NCH):
                            for qt in range(4):
                                P.op("tensor", lambda e, c=c, qt=qt, par=par: e.matmul(
                                    ps[qt][:],
                                    wstr[:, par * 1024 + c * 128: par * 1024 + (c + 1) * 128],
                                    xhat[:, c * T + qt * 512: c * T + (qt + 1) * 512],
                                    start=(c == 0), stop=False),
                                    reads=["FW%d" % par, "xh%d" % c, "PS%d" % qt],
                                    writes=["PS%d" % qt])
                        for qt in range(4):
                            P.op("tensor", lambda e, ot=ot, qt=qt, idx=idx: e.matmul(
                                ps[qt][:],
                                lbcur[0:32, ot * 128:(ot + 1) * 128],
                                ubuf[0:32, qt * 512:(qt + 1) * 512],
                                start=False, stop=True),
                                reads=["FU", "FLB", "PS%d" % qt], writes=["PS%d" % qt])
                        bcol = fbc[:, idx * 8 + ot: idx * 8 + ot + 1]
                        if ln_after:
                            for qt in range(4):
                                P.op("scalar", lambda e, ot=ot, qt=qt, bcol=bcol: e.activation(
                                    yb[:, ot * T + qt * 512: ot * T + (qt + 1) * 512],
                                    ps[qt][:], AF.Identity, bias=bcol),
                                    reads=["PS%d" % qt], writes=["FY%d" % ot])
                                P.op("scalar", lambda e, ot=ot, qt=qt, bcol=bcol: e.activation(
                                    sqb[:, qt * 512:(qt + 1) * 512],
                                    ps[qt][:], AF.Square, bias=bcol),
                                    reads=["PS%d" % qt], writes=["sqb", "PS%d" % qt])
                            for qt in range(4):
                                P.op("tensor", lambda e, ot=ot, qt=qt: e.matmul(
                                    ps[4][0:4, :], stat4[:, 4 * qt: 4 * qt + 4],
                                    yb[:, ot * T + qt * 512: ot * T + (qt + 1) * 512],
                                    start=(ot == 0 and qt == 0), stop=(ot == NCH - 1)),
                                    reads=["FY%d" % ot, "PS4"], writes=["PS4"])
                                P.op("tensor", lambda e, ot=ot, qt=qt: e.matmul(
                                    ps[5][0:4, :], stat4[:, 4 * qt: 4 * qt + 4],
                                    sqb[:, qt * 512:(qt + 1) * 512],
                                    start=(ot == 0 and qt == 0), stop=(ot == NCH - 1)),
                                    reads=["sqb", "PS5"], writes=["PS5"])
                        else:
                            for qt in range(4):
                                P.op("vector", lambda e, ot=ot, qt=qt, bcol=bcol: e.scalar_tensor_tensor(
                                    hbuf[:, ot * T + qt * 512: ot * T + (qt + 1) * 512],
                                    ps[qt][:], bcol,
                                    hbuf[:, ot * T + qt * 512: ot * T + (qt + 1) * 512],
                                    ALU.add, ALU.add),
                                    reads=["PS%d" % qt, "FH%d" % ot],
                                    writes=["FH%d" % ot, "PS%d" % qt])
                    if not ln_after:
                        return
                    P.op("scalar", lambda e: e.copy(stf[0:4, 0:512], ps[4][0:4, :]),
                         reads=["PS4"], writes=["scr", "PS4"])
                    P.op("vector", lambda e: e.tensor_tensor(
                        stf[0:4, 512:1024], stf[0:4, 0:512], stf[0:4, 0:512], ALU.mult),
                        reads=["scr"], writes=["scr"])
                    P.op("vector", lambda e: e.tensor_tensor(
                        stf[0:4, 512:1024], ps[5][0:4, :], stf[0:4, 512:1024],
                        ALU.subtract),
                        reads=["scr", "PS5"], writes=["scr", "PS5"])
                    P.op("scalar", lambda e: e.activation(
                        stf[0:4, 512:1024], stf[0:4, 512:1024], AF.Ln,
                        bias=epsc[0:4, 0:1]),
                        reads=["scr"], writes=["scr"])
                    P.op("scalar", lambda e: e.activation(
                        stf[0:4, 1024:1536], stf[0:4, 512:1024], AF.Exp, scale=-0.5),
                        reads=["scr"], writes=["scr"])
                    P.op("vector", lambda e: e.tensor_copy(stfb[0:4, 0:512], stf[0:4, 0:512]),
                         reads=["scr"], writes=["FSB"])
                    P.op("vector", lambda e: e.tensor_copy(stfb[0:4, 512:1024],
                                                           stf[0:4, 1024:1536]),
                         reads=["scr"], writes=["FSB"])
                    for qt in range(4):
                        P.op("tensor", lambda e, qt=qt: e.matmul(
                            ps[6][:], sel4[0:4, qt * 128:(qt + 1) * 128],
                            stfb[0:4, 0:512], start=True, stop=True),
                            reads=["FSB", "PS6"], writes=["PS6"])
                        P.op("vector", lambda e: e.tensor_copy(murep[:], ps[6][:]),
                             reads=["PS6"], writes=["MR", "PS6"])
                        P.op("tensor", lambda e, qt=qt: e.matmul(
                            ps[6][:], sel4[0:4, qt * 128:(qt + 1) * 128],
                            stfb[0:4, 512:1024], start=True, stop=True),
                            reads=["FSB", "PS6"], writes=["PS6"])
                        P.op("vector", lambda e: e.tensor_copy(rsrep[:], ps[6][:]),
                             reads=["PS6"], writes=["RR2", "PS6"])
                        for ot in range(NCH):
                            sl = slice(ot * T + qt * 512, ot * T + (qt + 1) * 512)
                            P.op("vector", lambda e, sl=sl: e.tensor_tensor(
                                yb[:, sl], yb[:, sl], murep[:], ALU.subtract),
                                reads=["FY%d" % ot, "MR"], writes=["FY%d" % ot])
                            P.op("vector", lambda e, sl=sl: e.tensor_tensor(
                                yb[:, sl], yb[:, sl], rsrep[:], ALU.mult),
                                reads=["FY%d" % ot, "RR2"], writes=["FY%d" % ot])
                            P.op("scalar", lambda e, sl=sl, ot=ot, blk=blk: e.activation(
                                xhat[:, sl], yb[:, sl], AF.Gelu,
                                bias=btc[:, blk * 8 + ot: blk * 8 + ot + 1],
                                scale=gmc[:, blk * 8 + ot: blk * 8 + ot + 1]),
                                reads=["FY%d" % ot], writes=["xh%d" % ot])

                for b in range(NB):
                    if b == 0 and probe:
                        dump("F", 8, xin[0, ts(0), :], ["XOUT"])
                    ln_phase(xin, b)
                    if b == 0 and probe:
                        dump("H", 12, xhat[:, 0:T], ["xh0"])
                    for c in range(NCH):
                        P.op("vector", lambda e, c=c: e.tensor_copy(
                            hbuf[:, ts(c, T)], xhat[:, ts(c, T)]),
                            reads=["xh%d" % c], writes=["FH%d" % c])
                    for k in range(3):
                        linear(2 * k, k, True)
                        if b == 0 and k == 0 and probe:
                            dump("H", 13, yb.ap()[:, 0:T], ["FY0"])
                            dump("F", 7, stf[0:4, 0:1536], ["scr"], w=1536, p=4)
                            dump("H", 14, xhat[:, 0:T], ["xh0"])
                            dump("H", 15, ubuf[0:32, 0:T], ["FU"], p=32)
                        linear(2 * k + 1, k, False)
                        if b == 0 and k == 0 and probe:
                            dump("F", 6, hbuf.ap()[:, 0:T], ["FH0"])
                        if k < 2:
                            for c in range(NCH):
                                P.op("scalar", lambda e, c=c: e.copy(
                                    xhat[:, ts(c, T)], hbuf[:, ts(c, T)]),
                                    reads=["FH%d" % c], writes=["xh%d" % c])
                    for et in range(NCH):
                        for j in range(4):
                            fq_in = "fb:%d" % (4 + j)
                            fq_out = "fb:%d" % j
                            dma(fb[:, T + j * 512: T + (j + 1) * 512],
                                xin[b, ts(et), j * 512:(j + 1) * 512], writes=[fq_in])
                            P.op("vector", lambda e, et=et, j=j: e.tensor_tensor(
                                fb[:, j * 512:(j + 1) * 512],
                                hbuf[:, et * T + j * 512: et * T + (j + 1) * 512],
                                fb[:, T + j * 512: T + (j + 1) * 512], ALU.add),
                                reads=["FH%d" % et, fq_in], writes=[fq_out])
                            dma(xout[b, ts(et), j * 512:(j + 1) * 512],
                                fb[:, j * 512:(j + 1) * 512],
                                reads=[fq_out], writes=["XOUT"])
                    P.barrier()
                P.default_fd = 0
            P.barrier()

        def scan(xin, xout):
            ses = contextlib.ExitStack()
            with ses:
                wdb = ses.enter_context(nc.sbuf_tensor("wdb", [128, NCH * 128], BF16))
                wob2 = ses.enter_context(nc.sbuf_tensor("wob2", [128, NCH * 128], BF16))
                na = ses.enter_context(nc.sbuf_tensor("na", [128, 128], F32))
                bp = ses.enter_context(nc.sbuf_tensor("bp", [128, 128], F32))
                df = ses.enter_context(nc.sbuf_tensor("df", [128, T], F32))
                uf = ses.enter_context(nc.sbuf_tensor("uf", [128, T], F32))
                Af = ses.enter_context(nc.sbuf_tensor("Af", [128, 4 * T], F32))
                Bf = ses.enter_context(nc.sbuf_tensor("Bf", [128, 3 * T], BF16))
                yp = ses.enter_context(nc.sbuf_tensor("yp", [128, 4 * T], BF16))
                stt_ = ses.enter_context(nc.sbuf_tensor("stt_", [128, 2 * T], BF16))
                yT = ses.enter_context(nc.sbuf_tensor("yT", [128, NCH * T], BF16))

                for c in range(NCH):
                    dma(na[:, c * 16:(c + 1) * 16], alog[ts(c), :], writes=["na"])
                    dma(bp[:, c * 16:(c + 1) * 16], bpar[ts(c), :], writes=["bp"])
                P.op("scalar", lambda e: e.activation(na[:], na[:], AF.Exp),
                     reads=["na"], writes=["na"])
                P.op("scalar", lambda e: e.activation(na[:], na[:], AF.Ln, bias=1.0),
                     reads=["na"], writes=["na"])
                P.op("vector", lambda e: e.tensor_scalar(na[:], na[:], -1.0, -1e-4,
                                                         ALU.mult, ALU.add),
                     reads=["na"], writes=["na"])
                P.default_fd = 512

                def sload(dst, src_ap, et, tag):
                    for c in range(NCH):
                        wfq = "fb:%d" % (4 + c % 2)
                        dma(fb[:, T + (c % 2) * 512: T + (c % 2) * 512 + 128],
                            src_ap[ts(c), ts(et)], writes=[wfq])
                        P.op("scalar", lambda e, c=c: e.copy(
                            dst[:, c * 128:(c + 1) * 128],
                            fb[:, T + (c % 2) * 512: T + (c % 2) * 512 + 128]),
                            reads=[wfq], writes=[tag])
                for b in range(NB):
                    ln_phase(xin, b)
                    for et in range(NCH):
                        sload(wdb, wdT, et, "WD")
                        for c in range(NCH):
                            for qt in range(4):
                                P.op("tensor", lambda e, et=et, c=c, qt=qt: e.matmul(
                                    ps[qt][:],
                                    wdb[:, c * 128:(c + 1) * 128],
                                    xhat[:, c * T + qt * 512: c * T + (qt + 1) * 512],
                                    start=(c == 0), stop=(c == NCH - 1)),
                                    reads=["WD", "xh%d" % c, "PS%d" % qt],
                                    writes=["PS%d" % qt])
                        for qt in range(4):
                            P.op("scalar", lambda e, et=et, qt=qt: e.activation(
                                df[:, qt * 512:(qt + 1) * 512], ps[qt][:], AF.Exp,
                                bias=bdcol[:, et:et + 1]),
                                reads=["PS%d" % qt], writes=["SD", "PS%d" % qt])
                            P.op("scalar", lambda e, qt=qt: e.activation(
                                df[:, qt * 512:(qt + 1) * 512],
                                df[:, qt * 512:(qt + 1) * 512], AF.Ln, bias=1.0),
                                reads=["SD"], writes=["SD"])
                        P.op("vector", lambda e, et=et: e.tensor_tensor(
                            uf[:], df[:], xhat[:, ts(et, T)], ALU.mult),
                            reads=["SD", "xh%d" % et], writes=["SU"])
                        for s in range(16):
                            pa = s % 3
                            paA = s % 4
                            A = Af[:, paA * T:(paA + 1) * T]
                            Bv = Bf[:, pa * T:(pa + 1) * T]
                            nac = na[:, et * 16 + s: et * 16 + s + 1]
                            bpc = bp[:, et * 16 + s: et * 16 + s + 1]
                            g, gi = divmod(s, 4)
                            ypg = yp[:, g * T:(g + 1) * T]
                            P.op("scalar", lambda e, nac=nac, A=A: e.activation(
                                A, df[:], AF.Exp, scale=nac),
                                reads=["SD", "na"], writes=["SA%d" % paA])
                            P.op("scalar", lambda e, bpc=bpc, Bv=Bv: e.activation(
                                Bv, uf[:], AF.Identity, scale=bpc),
                                reads=["SU", "bp"], writes=["SB%d" % pa])
                            if gi == 0:
                                P.op("vector", lambda e, A=A, Bv=Bv, ypg=ypg: e.tensor_tensor_scan(
                                    ypg, A, Bv, 0.0, ALU.mult, ALU.add),
                                    reads=["SA%d" % paA, "SB%d" % pa], writes=["YP%d" % g])
                            else:
                                sp2 = s % 2
                                st2 = stt_[:, sp2 * T:(sp2 + 1) * T]
                                P.op("vector", lambda e, A=A, Bv=Bv, st2=st2: e.tensor_tensor_scan(
                                    st2, A, Bv, 0.0, ALU.mult, ALU.add),
                                    reads=["SA%d" % paA, "SB%d" % pa], writes=["SS%d" % sp2])
                                P.op("gpsimd", lambda e, st2=st2, ypg=ypg: e.tensor_tensor(
                                    ypg, ypg, st2, ALU.add),
                                    reads=["YP%d" % g, "SS%d" % sp2],
                                    writes=["YP%d" % g])
                        P.op("vector", lambda e: e.tensor_tensor(
                            yp[:, 0:T], yp[:, 0:T], yp[:, T:2 * T], ALU.add),
                            reads=["YP0", "YP1"], writes=["YP0"])
                        P.op("vector", lambda e: e.tensor_tensor(
                            yp[:, 2 * T:3 * T], yp[:, 2 * T:3 * T],
                            yp[:, 3 * T:4 * T], ALU.add),
                            reads=["YP2", "YP3"], writes=["YP2"])
                        P.op("vector", lambda e, et=et: e.tensor_tensor(
                            yT[:, ts(et, T)], yp[:, 0:T], yp[:, 2 * T:3 * T],
                            ALU.add),
                            reads=["YP0", "YP2"], writes=["yT%d" % et])
                    for et in range(NCH):
                        sload(wob2, wobT, et, "WB")
                        for j in range(4):
                            pid = 4 + j
                            for c in range(NCH):
                                P.op("tensor", lambda e, et=et, j=j, c=c, pid=pid: e.matmul(
                                    ps[pid][:],
                                    wob2[:, c * 128:(c + 1) * 128],
                                    yT[:, c * T + j * 512: c * T + (j + 1) * 512],
                                    start=(c == 0), stop=(c == NCH - 1)),
                                    reads=["WB", "yT%d" % c, "PS%d" % pid],
                                    writes=["PS%d" % pid])
                            fq_in = "fb:%d" % (4 + j)
                            fq_out = "fb:%d" % j
                            dma(fb[:, T + j * 512: T + (j + 1) * 512],
                                xin[b, ts(et), j * 512:(j + 1) * 512], writes=[fq_in])
                            P.op("vector", lambda e, et=et, j=j, pid=pid: e.scalar_tensor_tensor(
                                fb[:, j * 512:(j + 1) * 512], ps[pid][:],
                                bobcol[:, et:et + 1],
                                fb[:, T + j * 512: T + (j + 1) * 512],
                                ALU.add, ALU.add),
                                reads=["PS%d" % pid, fq_in], writes=[fq_out, "PS%d" % pid])
                            dma(xout[b, ts(et), j * 512:(j + 1) * 512],
                                fb[:, j * 512:(j + 1) * 512],
                                reads=[fq_out], writes=["XOUT"])
                    P.barrier()
                P.default_fd = 0
            P.barrier()

        cur = xT
        if n_sub >= 1:
            attn_a(cur, Xs[0])
            cur = Xs[0]
        if n_sub >= 2:
            ffn(cur, Xs[1])
            cur = Xs[1]
        if n_sub >= 3:
            scan(cur, Xs[0])
            cur = Xs[0]
        if n_sub >= 4:
            ffn(cur, out)
            cur = None
        def copy_hbm(dst_ap, src_ap, tagbase):
            for b in range(NB):
                for c in range(NCH):
                    fq = ["fb:%d" % q for q in range((c % 2) * 4, (c % 2) * 4 + 4)]
                    dma(fb[:, ts(c % 2, T)], src_ap[b, ts(c), :], writes=fq)
                    dma(dst_ap[b, ts(c), :], fb[:, ts(c % 2, T)],
                        reads=fq, writes=[tagbase])

        for s, ap in dbg.items():
            copy_hbm(ap, Xs[0] if s == 1 else cur, "DBG")
        if cur is not None:
            copy_hbm(out, cur, "OUT")
        P.op("vector", lambda e: e.engine_nop(), reads=["OUT", "DBG", "PRB", "XOUT"])

        with nc.allow_low_precision(reason="deliberate bf16 rounding at bf16-matmul noise level"):
            P.materialize(block, sems, dmapool)
    return nc


def _perm_qk():
    p = np.zeros(1024, dtype=np.int64)
    for r in range(512):
        p[r] = (r // 32) * 64 + (r % 32)
        p[512 + r] = (r // 32) * 64 + 32 + (r % 32)
    return p


def _cols(v):
    return np.ascontiguousarray(v.reshape(8, 128).T)


N_SUB = 4


def host_inputs(inp):
    perm = _perm_qk()
    Wq = np.asarray(inp["L0_attn_a_Wq"], np.float32)[:, perm]
    Wkv = np.asarray(inp["L0_attn_a_Wkv"], np.float32)
    Wk = Wkv[:, :1024][:, perm]
    Wv = Wkv[:, 1024:]
    bq = np.asarray(inp["L0_attn_a_bq"], np.float32)[perm]
    bkv = np.asarray(inp["L0_attn_a_bkv"], np.float32)
    bk = bkv[:1024][perm]
    bv = bkv[1024:]
    Woa = np.asarray(inp["L0_attn_a_Wo"], np.float32)
    boa = np.asarray(inp["L0_attn_a_bo"], np.float32)
    qw = np.asarray(inp["L0_ffn_qweights"], np.int32)
    sc = np.asarray(inp["L0_ffn_scales"], np.float32)
    screp = np.ascontiguousarray(np.repeat(sc, 16, axis=2).transpose(0, 2, 1))
    la = np.asarray(inp["L0_ffn_lora_a"], np.float32)
    lb = np.asarray(inp["L0_ffn_lora_b"], np.float32)
    fb_ = np.asarray(inp["L0_ffn_biases"], np.float32)
    gam = np.asarray(inp["L0_ffn_ln_gamma"], np.float32)
    bet = np.asarray(inp["L0_ffn_ln_beta"], np.float32)
    com = {
        "wq": np.ascontiguousarray(Wq), "wk": np.ascontiguousarray(Wk),
        "wv": np.ascontiguousarray(Wv), "woa": np.ascontiguousarray(Woa),
        "bqc": _cols(bq), "bkc": _cols(bk),
        "bvr": np.ascontiguousarray(bv[None, :]), "boac": _cols(boa),
        "qT": np.ascontiguousarray(qw.transpose(0, 2, 1)),
        "screp": screp,
        "laT": np.ascontiguousarray(la.transpose(0, 2, 1)),
        "lbT": np.ascontiguousarray(lb.transpose(0, 2, 1)),
        "fbias": np.concatenate([_cols(fb_[l]) for l in range(6)], 1),
        "gamc": np.concatenate([_cols(gam[l]) for l in range(3)], 1),
        "betc": np.concatenate([_cols(bet[l]) for l in range(3)], 1),
        "wdT": np.ascontiguousarray(np.asarray(inp["L0_attn_b_Wd"], np.float32).T),
        "wobT": np.ascontiguousarray(np.asarray(inp["L0_attn_b_Wo"], np.float32).T),
        "bdc": _cols(np.asarray(inp["L0_attn_b_bd"], np.float32)),
        "bobc": _cols(np.asarray(inp["L0_attn_b_bo"], np.float32)),
        "alog": np.ascontiguousarray(np.asarray(inp["L0_attn_b_a_log"], np.float32)),
        "bpar": np.ascontiguousarray(np.asarray(inp["L0_attn_b_b_param"], np.float32)),
    }
    return com


def kernel(**inp):
    x = np.asarray(inp["L0_attn_a_x"], dtype=np.float32)
    com = host_inputs(inp)
    nc = build_nc(n_sub=N_SUB)
    in_maps = []
    for c in range(8):
        m = dict(com)
        m["xT"] = np.ascontiguousarray(x[2 * c: 2 * c + 2].transpose(0, 2, 1))
        in_maps.append(m)
    res = run_bass_kernel_spmd(nc, in_maps, core_ids=list(range(8)))
    outs = [np.asarray(res.results[c]["outT"]).transpose(0, 2, 1)
            for c in range(8)]
    return np.ascontiguousarray(np.concatenate(outs, axis=0))


# version 44
# speedup vs baseline: 1.0709x; 1.0709x over previous
import contextlib
import numpy as np
import ml_dtypes
import concourse.bass as bass
import concourse.mybir as mybir
from concourse.bass_utils import run_bass_kernel_spmd

F32 = mybir.dt.float32
BF16 = mybir.dt.bfloat16
I32 = mybir.dt.int32
AF = mybir.ActivationFunctionType
ALU = mybir.AluOpType
AXL = mybir.AxisListType

D = 1024
T = 2048
NB = 2
NCH = 8
H = 16
EPS = 1e-5
LN1E4 = float(np.log(10000.0))
ENGINES = ("tensor", "vector", "scalar", "gpsimd", "sync")


def ts(i, n=128):
    return slice(i * n, (i + 1) * n)


class Prog:

    def __init__(self):
        self.ops = []
        self.fds = []
        self.last_w = {}
        self.readers = {}
        self.bar = set()
        self.default_fd = 0

    import contextlib as _ctx

    @_ctx.contextmanager
    def fd(self, v):
        old = self.default_fd
        self.default_fd = v
        try:
            yield
        finally:
            self.default_fd = old

    def op(self, engine, fn, reads=(), writes=(), fd=None):
        self.fds.append(self.default_fd if fd is None else fd)
        deps = set(self.bar)
        for r in reads:
            if r in self.last_w:
                deps.add(self.last_w[r])
        for w in writes:
            if w in self.last_w:
                deps.add(self.last_w[w])
            deps.update(self.readers.get(w, ()))
        i = len(self.ops)
        self.ops.append((engine, fn, deps))
        for r in reads:
            self.readers.setdefault(r, []).append(i)
        for w in writes:
            self.last_w[w] = i
            self.readers[w] = []
        return i

    def barrier(self):
        tails = {}
        for i, (en, _, _) in enumerate(self.ops):
            tails[en] = i
        self.bar = set(tails.values())

    def materialize(self, block, sems, dmapool):
        ops = self.ops
        n = len(ops)
        needed = [False] * n
        for en, fn, deps in ops:
            for d in deps:
                if ops[d][0] != en:
                    needed[d] = True
        for i, (en, fn, deps) in enumerate(ops):
            if en == "sync":
                needed[i] = True
        if n:
            needed[n - 1] = True
        val = [0] * n
        slot = [None] * n
        cnt = {e: 0 for e in ENGINES}
        NSLOT = len(dmapool)
        uses = [0] * NSLOT
        dma_i = 0
        for i, (en, fn, deps) in enumerate(ops):
            if en == "sync":
                s = dma_i % NSLOT
                slot[i] = s
                uses[s] += 1
                val[i] = 16 * uses[s]
                dma_i += 1
            else:
                if needed[i]:
                    cnt[en] += 1
                val[i] = cnt[en]
        fds = self.fds

        def section(eng_obj, name):
            hw = {}
            last_drain = -1
            do_drain = name in ("vector", "scalar", "gpsimd")
            my_dma = 0
            for i, (en, fn, deps) in enumerate(ops):
                if en != name:
                    continue
                byen = {}
                need_drain = False
                for d in deps:
                    den = ops[d][0]
                    if den == "sync":
                        k = ("D", slot[d])
                        byen[k] = max(byen.get(k, 0), val[d])
                    elif den != name:
                        byen[den] = max(byen.get(den, 0), val[d])
                    elif d >= last_drain and min(fds[i], fds[d]) < 256:
                        need_drain = True
                if name == "sync":
                    s = slot[i]
                    if val[i] > 16:
                        k = ("D", s)
                        byen[k] = max(byen.get(k, 0), val[i] - 16)
                for k, v in sorted(byen.items(), key=str):
                    if v > hw.get(k, 0):
                        if isinstance(k, tuple):
                            eng_obj.wait_ge(dmapool[k[1]], v)
                        else:
                            eng_obj.wait_ge(sems[k], v)
                        hw[k] = v
                if do_drain and need_drain:
                    eng_obj.drain()
                    last_drain = i
                ins = fn(eng_obj)
                if name == "sync":
                    ins.then_inc(dmapool[slot[i]], 16)
                elif needed[i]:
                    ins.then_inc(sems[name], 1)

        @block.tensor
        def _(e):
            section(e, "tensor")

        @block.vector
        def _(e):
            section(e, "vector")

        @block.scalar
        def _(e):
            section(e, "scalar")

        @block.gpsimd
        def _(e):
            section(e, "gpsimd")

        @block.sync
        def _(e):
            section(e, "sync")


def build_nc(n_sub=4, debug=(), probe=False):
    nc = bass.Bass("TRN2", target_bir_lowering=False, debug=False)
    P = Prog()

    def din(name, shape, dt=F32):
        return nc.dram_tensor(name, list(shape), dt, kind="ExternalInput").ap()

    xT = din("xT", [NB, D, T])
    wq = din("wq", [D, D])
    wk = din("wk", [D, D])
    wv = din("wv", [D, D])
    woa = din("woa", [D, D])
    bqc = din("bqc", [128, 8])
    bkc = din("bkc", [128, 8])
    bvr = din("bvr", [1, D])
    boac = din("boac", [128, 8])
    qT = din("qT", [6, D, D], I32)
    screp = din("screp", [6, D, D])
    laT = din("laT", [6, D, 32])
    lbT = din("lbT", [6, 32, D])
    fbias = din("fbias", [128, 48])
    gamc = din("gamc", [128, 24])
    betc = din("betc", [128, 24])
    wdT = din("wdT", [D, D])
    wobT = din("wobT", [D, D])
    bdc = din("bdc", [128, 8])
    bobc = din("bobc", [128, 8])
    alog = din("alog", [D, 16])
    bpar = din("bpar", [D, 16])

    out = nc.dram_tensor("outT", [NB, D, T], F32, kind="ExternalOutput").ap()
    Xs = [nc.dram_tensor("Xscr%d" % i, [NB, D, T], F32, kind="Internal").ap()
          for i in range(2)]
    wstash = nc.dram_tensor("wstash", [6, D, D], BF16, kind="Internal").ap()
    wbstash = nc.dram_tensor("wbstash", [6, 128, NCH * 1024], BF16, kind="Internal").ap()
    probeH = probeF = None
    if probe:
        probeH = nc.dram_tensor("probeH", [20, 128, T], BF16, kind="ExternalOutput").ap()
        probeF = nc.dram_tensor("probeF", [9, 128, T], F32, kind="ExternalOutput").ap()
    dbg = {}
    for s in debug:
        dbg[s] = nc.dram_tensor("dbgX%d" % s, [NB, D, T], F32,
                                kind="ExternalOutput").ap()

    es = contextlib.ExitStack()
    with es:
        def sb(name, shape, dt):
            return es.enter_context(nc.sbuf_tensor(name, shape, dt))

        sems = {e: es.enter_context(nc.semaphore("sem_" + e)) for e in ENGINES}
        dmapool = [es.enter_context(nc.semaphore("dmap%d" % i)) for i in range(64)]
        block = es.enter_context(nc.Block())

        ones_stat = sb("ones_stat", [128, 128], BF16)
        ones_full = sb("ones_full", [128, 128], BF16)
        bo4 = sb("bo4", [128, 4], BF16)
        biasq = sb("biasq", [128, 8], F32)
        biask = sb("biask", [128, 8], F32)
        biasoa = sb("biasoa", [128, 8], F32)
        nrm = sb("nrm", [128, 8], F32)
        epsc = sb("epsc", [128, 2], F32)
        id4 = sb("id4", [128, 4], BF16)
        dbt = sb("dbt", [128, 8], F32)
        sel4 = sb("sel4", [128, 512], BF16)
        lastore = sb("lastore", [128, 1536], BF16)
        fbc = sb("fbc", [128, 48], F32)
        gmc = sb("gmc", [128, 24], F32)
        btc = sb("btc", [128, 24], F32)
        stat4 = sb("stat4", [128, 16], BF16)
        bdcol = sb("bdcol", [128, 8], F32)
        bobcol = sb("bobcol", [128, 8], F32)

        srep = sb("srep", [128, T], BF16)
        msrep = sb("msrep", [128, T], BF16)
        sqb = sb("sqb", [128, T], BF16)
        statf = sb("statf", [128, T + 512], F32)
        fb = sb("fb", [128, 2 * T], F32)
        scr = sb("scr", [128, T], F32)
        xhat = sb("xhat", [128, NCH * T], BF16)

        ps = [es.enter_context(nc.psum_tensor("ps%d" % i, [128, 512], F32))
              for i in range(8)]

        def dma(dst, src, reads=(), writes=()):
            return P.op("sync", lambda e: e.dma_start(out=dst, in_=src),
                        reads=reads, writes=writes)

        def dump(which, idx, ap, reads, w=None, p=128):
            if not probe:
                return
            tgt = probeH if which == "H" else probeF
            dst = tgt[idx]
            if w is not None or p != 128:
                dst = dst[0:p, 0:(w if w is not None else T)]
            dma(dst, ap, reads=list(reads), writes=["PRB"])

        PRE = ["pre"]

        def pre(eng, fn):
            P.op(eng, fn, reads=PRE, writes=PRE)

        pre("vector", lambda e: e.memset(ones_stat[:], 1.0 / 1024.0))
        pre("vector", lambda e: e.memset(ones_full[:], 1.0))
        pre("vector", lambda e: e.memset(epsc[:, 0:1], EPS))
        pre("vector", lambda e: e.memset(epsc[:, 1:2], 1e-20))
        pre("vector", lambda e: e.memset(bo4[:], 0.0))
        for h4 in range(4):
            pre("vector", lambda e, h4=h4: e.memset(bo4[ts(h4, 32), h4:h4 + 1], 1.0))

        pre("gpsimd", lambda e: e.iota(statf[:, T:T + 4], [[1, 4]], base=0,
                                       channel_multiplier=-1,
                                       allow_small_or_imprecise_dtypes=True))
        pre("vector", lambda e: e.tensor_scalar(id4[:], statf[:, T:T + 4],
                                                0.0, None, ALU.is_equal))
        pre("gpsimd", lambda e: e.iota(scr[:, 512:1024], [[1, 4], [0, 128]], base=0,
                                       channel_multiplier=-1,
                                       allow_small_or_imprecise_dtypes=True))
        pre("vector", lambda e: e.tensor_scalar(sel4[:], scr[:, 512:1024], 0.0,
                                                None, ALU.is_equal))
        pre("vector", lambda e: e.memset(stat4[:], 0.0))
        for j in range(4):
            pre("vector", lambda e, j=j: e.memset(stat4[:, 4 * j + j: 4 * j + j + 1],
                                                  1.0 / 1024.0))
        dma(fbc[:], fbias, reads=PRE, writes=PRE)
        dma(gmc[:], gamc, reads=PRE, writes=PRE)
        dma(btc[:], betc, reads=PRE, writes=PRE)
        dma(bdcol[:], bdc, reads=PRE, writes=PRE)
        dma(bobcol[:], bobc, reads=PRE, writes=PRE)
        for l in range(6):
            for c in range(NCH):
                dma(scr[:, 0:32], laT[l, ts(c), :], reads=PRE, writes=PRE)
                pre("vector", lambda e, l=l, c=c: e.tensor_copy(
                    lastore[:, l * 256 + c * 32: l * 256 + (c + 1) * 32],
                    scr[:, 0:32]))
        P.default_fd = 512
        for l in range(6):
            for c in range(NCH):
                p_ = c % 2
                o_ = p_ * 1024
                dma(fb[:, o_:o_ + 1024], screp[l, ts(c), :], writes=["dqa%d" % p_])
                dma(statf[:, o_:o_ + 1024].bitcast(I32), qT[l, ts(c), :],
                    writes=["dqb%d" % p_])
                P.op("vector", lambda e, o_=o_: e.tensor_scalar(
                    scr[:, o_:o_ + 1024], statf[:, o_:o_ + 1024].bitcast(I32),
                    1.0 / 7.5, -1.0, ALU.mult, ALU.add),
                    reads=["dqb%d" % p_], writes=["dqc%d" % p_, "dqb%d" % p_])
                P.op("vector", lambda e, o_=o_: e.tensor_tensor(
                    sqb[:, o_:o_ + 1024], scr[:, o_:o_ + 1024], fb[:, o_:o_ + 1024],
                    ALU.mult),
                    reads=["dqc%d" % p_, "dqa%d" % p_],
                    writes=["dqd%d" % p_, "dqa%d" % p_, "dqc%d" % p_])
                dma(wstash[l, ts(c), :], sqb[:, o_:o_ + 1024],
                    reads=["dqd%d" % p_], writes=["dqd%d" % p_])
        for wi, wap in enumerate([wq, wk, wv, woa, wdT, wobT]):
            wsrc2 = wap.rearrange("(c p) f -> p c f", p=128)
            for c2 in range(0, NCH, 2):
                p_ = (c2 // 2) % 2
                o_ = p_ * 2048
                dma(fb[:, o_:o_ + 2048].rearrange("p (c f) -> p c f", c=2),
                    wsrc2[:, c2:c2 + 2, :], writes=["dqa%d" % p_])
                P.op("vector", lambda e, o_=o_: e.tensor_copy(
                    sqb[:, 0:2048], fb[:, o_:o_ + 2048]),
                    reads=["dqa%d" % p_], writes=["dqd0", "dqa%d" % p_])
                dma(wbstash[wi, :, c2 * 1024:(c2 + 2) * 1024], sqb[:, 0:2048],
                    reads=["dqd0"], writes=["dqd0"])
        P.op("vector", lambda e: e.engine_nop(),
             reads=["dqa0", "dqa1", "dqb0", "dqb1", "dqd0", "dqd1"], writes=PRE)
        P.default_fd = 0
        pre("gpsimd", lambda e: e.iota(scr[:, 0:1], [[1, 1]], base=0,
                                       channel_multiplier=1,
                                       allow_small_or_imprecise_dtypes=True))
        pre("vector", lambda e: e.tensor_scalar(scr[:, 4:5], scr[:, 0:1], 32.0, None, ALU.is_ge))
        pre("vector", lambda e: e.tensor_scalar(scr[:, 5:6], scr[:, 0:1], 64.0, None, ALU.is_ge))
        pre("vector", lambda e: e.tensor_scalar(scr[:, 6:7], scr[:, 0:1], 96.0, None, ALU.is_ge))
        pre("vector", lambda e: e.tensor_tensor(scr[:, 4:5], scr[:, 4:5], scr[:, 5:6], ALU.add))
        pre("vector", lambda e: e.tensor_tensor(scr[:, 4:5], scr[:, 4:5], scr[:, 6:7], ALU.add))
        pre("vector", lambda e: e.scalar_tensor_tensor(scr[:, 1:2], scr[:, 4:5], -32.0,
                                                       scr[:, 0:1], ALU.mult, ALU.add))
        pre("scalar", lambda e: e.activation(scr[:, 2:3], scr[:, 1:2], AF.Exp,
                                             scale=-LN1E4 / 32.0))
        fcol, f2 = scr[:, 2:3], scr[:, 3:4]
        C = fb
        S = statf
        pre("vector", lambda e: e.tensor_tensor(f2, fcol, fcol, ALU.mult))
        pre("vector", lambda e: e.tensor_scalar(scr[:, 4:5], f2, -1 / 30., 1., ALU.mult, ALU.add))
        pre("vector", lambda e: e.tensor_tensor(scr[:, 5:6], scr[:, 4:5], f2, ALU.mult))
        pre("vector", lambda e: e.tensor_scalar(scr[:, 5:6], scr[:, 5:6], -1 / 12., 1., ALU.mult, ALU.add))
        pre("vector", lambda e: e.tensor_tensor(scr[:, 6:7], scr[:, 5:6], f2, ALU.mult))
        pre("vector", lambda e: e.tensor_scalar(C[:, 1:2], scr[:, 6:7], -0.5, 1., ALU.mult, ALU.add))
        pre("vector", lambda e: e.tensor_scalar(scr[:, 4:5], f2, -1 / 42., 1., ALU.mult, ALU.add))
        pre("vector", lambda e: e.tensor_tensor(scr[:, 5:6], scr[:, 4:5], f2, ALU.mult))
        pre("vector", lambda e: e.tensor_scalar(scr[:, 5:6], scr[:, 5:6], -1 / 20., 1., ALU.mult, ALU.add))
        pre("vector", lambda e: e.tensor_tensor(scr[:, 6:7], scr[:, 5:6], f2, ALU.mult))
        pre("vector", lambda e: e.tensor_scalar(scr[:, 6:7], scr[:, 6:7], -1 / 6., 1., ALU.mult, ALU.add))
        pre("vector", lambda e: e.tensor_tensor(S[:, 1:2], scr[:, 6:7], fcol, ALU.mult))
        pre("vector", lambda e: e.memset(C[:, 0:1], 1.0))
        pre("vector", lambda e: e.memset(S[:, 0:1], 0.0))
        if probe:
            pre("vector", lambda e: e.tensor_copy(dbt[:, 4:5], fcol))
            pre("vector", lambda e: e.tensor_copy(dbt[:, 5:6], f2))
            pre("vector", lambda e: e.tensor_copy(dbt[:, 6:7], C[:, 1:2]))
            pre("vector", lambda e: e.tensor_copy(dbt[:, 7:8], S[:, 1:2]))
        known = 2
        while known < T:
            s = known - 1
            m = min(known - 1, T - known)
            pre("vector", lambda e, s=s, m=m, k=known: e.tensor_scalar(
                fb[:, T:T + m], C[:, 1:1 + m], C[:, s:s + 1], None, ALU.mult))
            pre("vector", lambda e, s=s, m=m, k=known: e.tensor_scalar(
                scr[:, 0:m], S[:, 1:1 + m], S[:, s:s + 1], None, ALU.mult))
            pre("vector", lambda e, s=s, m=m, k=known: e.tensor_scalar(
                scr[:, 1024:1024 + m], S[:, 1:1 + m], C[:, s:s + 1], None, ALU.mult))
            pre("vector", lambda e, s=s, m=m, k=known: e.tensor_scalar(
                fb[:, T + 1024:T + 1024 + m], C[:, 1:1 + m], S[:, s:s + 1], None, ALU.mult))
            pre("vector", lambda e, s=s, m=m, k=known: e.tensor_tensor(
                C[:, k:k + m], fb[:, T:T + m], scr[:, 0:m], ALU.subtract))
            pre("vector", lambda e, s=s, m=m, k=known: e.tensor_tensor(
                S[:, k:k + m], scr[:, 1024:1024 + m], fb[:, T + 1024:T + 1024 + m], ALU.add))
            known += m
        dma(biasq[:], bqc, reads=PRE, writes=PRE)
        dma(biask[:], bkc, reads=PRE, writes=PRE)
        dma(biasoa[:], boac, reads=PRE, writes=PRE)
        P.barrier()

        def ln_phase(xin, b):
            for c in range(NCH):
                fq = ["fb:%d" % q for q in range((c % 2) * 4, (c % 2) * 4 + 4)]
                dma(fb[:, ts(c % 2, T)], xin[b, ts(c), :], writes=fq)
                P.op("scalar",
                     lambda e, c=c: e.copy(xhat[:, ts(c, T)], fb[:, ts(c % 2, T)]),
                     reads=fq, writes=["xh%d" % c])
                P.op("scalar",
                     lambda e, c=c: e.activation(sqb[:], fb[:, ts(c % 2, T)], AF.Square),
                     reads=fq, writes=["sqb"])
                for j in range(4):
                    P.op("tensor",
                         lambda e, c=c, j=j: e.matmul(
                             ps[j][:], ones_stat[:],
                             xhat[:, c * T + 512 * j: c * T + 512 * (j + 1)],
                             start=(c == 0), stop=(c == NCH - 1)),
                         reads=["xh%d" % c, "PS%d" % j], writes=["PS%d" % j])
                    P.op("tensor",
                         lambda e, c=c, j=j: e.matmul(
                             ps[4 + j][:], ones_stat[:],
                             sqb[:, 512 * j: 512 * (j + 1)],
                             start=(c == 0), stop=(c == NCH - 1)),
                         reads=["sqb", "PS%d" % (4 + j)], writes=["PS%d" % (4 + j)])
            for j in range(4):
                sl = slice(512 * j, 512 * (j + 1))
                tmp = statf[:, T:T + 512]
                P.op("scalar", lambda e, j=j, sl=sl: e.copy(statf[:, sl], ps[j][:]),
                     reads=["PS%d" % j], writes=["statf", "PS%d" % j])
                P.op("vector", lambda e, sl=sl, tmp=tmp: e.tensor_tensor(
                    tmp, statf[:, sl], statf[:, sl], ALU.mult),
                    reads=["statf"], writes=["statf"])
                P.op("vector", lambda e, j=j, tmp=tmp: e.tensor_tensor(
                    tmp, ps[4 + j][:], tmp, ALU.subtract),
                    reads=["statf", "PS%d" % (4 + j)],
                    writes=["statf", "PS%d" % (4 + j)])
                P.op("scalar", lambda e, tmp=tmp: e.activation(tmp, tmp, AF.Ln, bias=epsc[:, 0:1]),
                     reads=["statf"], writes=["statf"])
                P.op("scalar", lambda e, sl=sl, tmp=tmp: e.activation(
                    srep[:, sl], tmp, AF.Exp, scale=-0.5),
                    reads=["statf"], writes=["srep"])
                P.op("vector", lambda e, sl=sl: e.tensor_tensor(
                    msrep[:, sl], statf[:, sl], srep[:, sl], ALU.mult),
                    reads=["statf", "srep"], writes=["msrep"])
            for c in range(NCH):
                P.op("vector", lambda e, c=c: e.tensor_tensor(
                    xhat[:, ts(c, T)], xhat[:, ts(c, T)], srep[:], ALU.mult),
                    reads=["xh%d" % c, "srep"], writes=["xh%d" % c])
                P.op("vector", lambda e, c=c: e.tensor_tensor(
                    xhat[:, ts(c, T)], xhat[:, ts(c, T)], msrep[:], ALU.subtract),
                    reads=["xh%d" % c, "msrep"], writes=["xh%d" % c])

        def attn_a(xin, xout):
            aes = contextlib.ExitStack()
            with aes:
                def asb(name, shape, dt):
                    return aes.enter_context(nc.sbuf_tensor(name, shape, dt))
                vstore = asb("vstore", [128, 16 * 1040], BF16)
                qk = asb("qk", [128, 4 * T], BF16)
                ptil = asb("ptil", [128, 4 * 512], BF16)
                otq = asb("otq", [128, NCH * T], BF16)
                osc = asb("osc", [128, 512], BF16)
                wvb = asb("wvb", [128, 4096], BF16)
                wqk = asb("wqk", [128, 4 * NCH * 128], BF16)
                cos_t = asb("cos_t", [128, T], BF16)
                sin_t = asb("sin_t", [128, T], BF16)
                masks = asb("masks", [128, 896], BF16)
                bvrow = asb("bvrow", [1, D], BF16)
                bct = asb("bct", [128, 8], BF16)
                selH = asb("selH", [128, 1024], BF16)
                denst = asb("denst", [128, T], BF16)
                rrb = asb("rrb", [128, T], BF16)

                P.op("vector", lambda e: e.tensor_copy(cos_t[:], fb[:, 0:T]),
                     reads=PRE + ["fb:0", "fb:1", "fb:2", "fb:3"], writes=["cos_t"])
                P.op("vector", lambda e: e.tensor_copy(sin_t[:], statf[:, 0:T]),
                     reads=PRE + ["statf"], writes=["sin_t"])
                P.op("gpsimd", lambda e: e.iota(scr[:, 0:896], [[1, 896]], base=-384,
                                                channel_multiplier=-1,
                                                allow_small_or_imprecise_dtypes=True),
                     reads=PRE + ["scr"], writes=["scr"])
                P.op("vector", lambda e: e.tensor_scalar(
                    masks[:], scr[:, 0:896], 0.0, None, ALU.is_ge),
                    reads=["scr"], writes=["masks"])
                dma(scr[0:1, 0:D], bvr, reads=PRE, writes=["scr"])
                P.op("vector", lambda e: e.tensor_copy(bvrow[:], scr[0:1, 0:D]),
                     reads=["scr"], writes=["bvrow"])
                P.op("gpsimd", lambda e: e.iota(scr[:, 0:1024],
                                                [[2, 8], [1, 2], [0, 64]], base=0,
                                                channel_multiplier=-1,
                                                allow_small_or_imprecise_dtypes=True),
                     reads=["scr"], writes=["scr"])
                P.op("vector", lambda e: e.tensor_scalar(
                    selH[:], scr[:, 0:1024], 0.0, None, ALU.is_equal),
                    reads=["scr"], writes=["selH"])

                def load_w(tag, wi, dst, c0, width, extra=()):
                    srcs = wbstash[wi].rearrange("p (c f) -> p c f", c=NCH)[
                        :, :, c0:c0 + width]
                    dsts = dst[:, 0:NCH * width].rearrange("p (c f) -> p c f", c=NCH)
                    dma(dsts, srcs, reads=list(extra), writes=[tag])

                for b in range(NB):
                    ln_phase(xin, b)
                    if b == 0:
                        dump("H", 0, xhat[:, 0:T], ["xh0"])
                        dump("H", 6, srep[:], ["srep"])
                        dump("H", 7, msrep[:], ["msrep"])
                        dump("F", 0, statf[:, 0:T], ["statf"])
                    P.op("vector", lambda e: e.memset(
                        vstore.ap().rearrange("p (t g c) -> p (t g) c",
                                              t=16, g=16, c=65)[:, :, 64:65], 1.0),
                        writes=["vones"])
                    for half in range(2):
                        load_w("wvbA", 2, wvb, half * 512, 512)
                        for tt in range(16):
                            pid = tt % 4
                            for c in range(NCH):
                                P.op("tensor", lambda e, tt=tt, c=c, pid=pid: e.matmul(
                                    ps[pid][:],
                                    xhat[:, c * T + tt * 128: c * T + (tt + 1) * 128],
                                    wvb[:, c * 512:(c + 1) * 512],
                                    start=(c == 0), stop=False),
                                    reads=["xh%d" % c, "wvbA", "PS%d" % pid],
                                    writes=["PS%d" % pid])
                            P.op("tensor", lambda e, half=half, pid=pid: e.matmul(
                                ps[pid][:], ones_full[0:1, 0:128],
                                bvrow[0:1, half * 512: half * 512 + 512],
                                start=False, stop=True),
                                reads=["PS%d" % pid, "bvrow"], writes=["PS%d" % pid])
                            P.op("scalar", lambda e, tt=tt, half=half, pid=pid: e.copy(
                                vstore.ap()[:, tt * 1040:(tt + 1) * 1040]
                                .rearrange("p (g c) -> p g c", g=16, c=65)
                                [:, half * 8:(half + 1) * 8, 0:64],
                                ps[pid][:]),
                                reads=["PS%d" % pid, "vones"],
                                writes=["vst%d" % tt, "PS%d" % pid])
                    if b == 0:
                        dump("H", 3, vstore.ap()[:, 0:T], ["vst0", "vst1"])
                        dump("H", 8, cos_t[:], ["cos_t"])
                        dump("H", 9, sin_t[:], ["sin_t"])
                    for cp in range(4):
                        for grp in range(2):
                            for wl, ot in enumerate([cp, 4 + cp]):
                                wr = grp * 2 + wl
                                load_w("wqkR%d" % wr, grp,
                                       wqk.ap()[:, wr * NCH * 128:(wr + 1) * NCH * 128],
                                       ot * 128, 128)
                            for wl in range(2):
                                wi = grp * 2 + wl
                                bias = biasq if grp == 0 else biask
                                bcol = cp if wl == 0 else 4 + cp
                                for j in range(4):
                                    pid = j
                                    for c in range(NCH):
                                        P.op("tensor", lambda e, wi=wi, j=j, c=c, pid=pid: e.matmul(
                                            ps[pid][:],
                                            wqk[:, (wi * NCH + c) * 128:(wi * NCH + c + 1) * 128],
                                            xhat[:, c * T + j * 512: c * T + (j + 1) * 512],
                                            start=(c == 0), stop=(c == NCH - 1)),
                                            reads=["wqkR%d" % wi, "xh%d" % c, "PS%d" % pid],
                                            writes=["PS%d" % pid])
                                    P.op("scalar", lambda e, wi=wi, j=j, pid=pid,
                                         bias=bias, bcol=bcol: e.activation(
                                             qk[:, wi * T + j * 512: wi * T + (j + 1) * 512],
                                             ps[pid][:], AF.Identity,
                                             bias=bias[:, bcol:bcol + 1]),
                                         reads=["PS%d" % pid], writes=["qk%d" % wi, "PS%d" % pid])
                        for base in (0, 2):
                            t1, t2 = "qk%d" % base, "qk%d" % (base + 1)
                            a1 = qk[:, base * T:(base + 1) * T]
                            a2 = qk[:, (base + 1) * T:(base + 2) * T]
                            P.op("vector", lambda e, a1=a1: e.tensor_tensor(
                                rrb[:, 0:T], a1, sin_t[:], ALU.mult),
                                reads=[t1], writes=["rscA"])
                            P.op("vector", lambda e, a2=a2: e.tensor_tensor(
                                sqb[:, 0:T], a2, sin_t[:], ALU.mult),
                                reads=[t2], writes=["sqb"])
                            P.op("vector", lambda e, a1=a1: e.tensor_tensor(
                                a1, a1, cos_t[:], ALU.mult), reads=[t1], writes=[t1])
                            P.op("vector", lambda e, a1=a1: e.tensor_tensor(
                                a1, a1, sqb[:, 0:T], ALU.subtract),
                                reads=[t1, "sqb"], writes=[t1])
                            P.op("vector", lambda e, a2=a2: e.tensor_tensor(
                                a2, a2, cos_t[:], ALU.mult), reads=[t2], writes=[t2])
                            P.op("vector", lambda e, a2=a2: e.tensor_tensor(
                                a2, a2, rrb[:, 0:T], ALU.add),
                                reads=[t2, "rscA"], writes=[t2])
                        if b == 0 and cp == 0:
                            dump("H", 1, qk[:, 0:T], ["qk0"])
                            dump("H", 2, qk[:, 2 * T:3 * T], ["qk2"])
                        P.default_fd = 1
                        for which, (b0, b1) in enumerate([(0, 1), (2, 3)]):
                            for j in range(4):
                                for wi in (b0, b1):
                                    P.op("scalar", lambda e, wi=wi, j=j: e.activation(
                                        sqb[:, 0:512],
                                        qk[:, wi * T + j * 512: wi * T + (j + 1) * 512],
                                        AF.Square),
                                        reads=["qk%d" % wi], writes=["sqb"], fd=512)
                                    P.op("tensor", lambda e, wi=wi, b0=b0, b1=b1: e.matmul(
                                        ps[4][0:4, :], bo4[:], sqb[:, 0:512],
                                        start=(wi == b0), stop=(wi == b1)),
                                        reads=["sqb", "PS4"], writes=["PS4"])
                                P.op("vector", lambda e, j=j: e.reduce_max(
                                    scr[0:4, j:j + 1], ps[4][0:4, :], axis=AXL.X),
                                    reads=["PS4"], writes=["scr", "PS4"])
                            P.op("vector", lambda e, which=which: e.reduce_max(
                                scr[0:4, 4 + which:5 + which], scr[0:4, 0:4], axis=AXL.X),
                                reads=["scr"], writes=["scr"])
                        P.op("vector", lambda e: e.tensor_tensor(
                            scr[0:4, 6:7], scr[0:4, 4:5], scr[0:4, 5:6], ALU.mult),
                            reads=["scr"], writes=["scr"])
                        if probe and cp == 0 and b == 0:
                            P.op("vector", lambda e: e.tensor_copy(dbt[0:4, 0:1], scr[0:4, 6:7]), reads=["scr"], writes=["dbt"])
                        P.op("scalar", lambda e: e.activation(
                            scr[0:4, 6:7], scr[0:4, 6:7], AF.Ln, bias=epsc[0:4, 1:2]),
                            reads=["scr"], writes=["scr"])
                        if probe and cp == 0 and b == 0:
                            P.op("vector", lambda e: e.tensor_copy(dbt[0:4, 1:2], scr[0:4, 6:7]), reads=["scr"], writes=["dbt"])
                        P.op("scalar", lambda e: e.activation(
                            scr[0:4, 6:7], scr[0:4, 6:7], AF.Exp, scale=0.5),
                            reads=["scr"], writes=["scr"])
                        if probe and cp == 0 and b == 0:
                            P.op("vector", lambda e: e.tensor_copy(dbt[0:4, 2:3], scr[0:4, 6:7]), reads=["scr"], writes=["dbt"])
                        P.op("vector", lambda e: e.tensor_scalar(
                            scr[0:4, 6:7], scr[0:4, 6:7], -0.125, None, ALU.mult),
                            reads=["scr"], writes=["scr"])
                        if probe and cp == 0 and b == 0:
                            P.op("vector", lambda e: e.tensor_copy(dbt[0:4, 3:4], scr[0:4, 6:7]), reads=["scr"], writes=["dbt"])
                        for _w in range(8):
                            P.op("tensor", lambda e: e.matmul(
                                ps[5][0:1, 4:512], ones_full[0:1, 0:1],
                                masks[0:1, 0:508], start=True, stop=True),
                                reads=[], writes=["WARM"])
                        P.op("vector", lambda e: e.tensor_scalar(
                            bct[0:4, 0:4], id4[0:4, 0:4], scr[0:4, 6:7],
                            None, ALU.mult),
                            reads=["scr"], writes=["bct"])
                        P.op("tensor", lambda e: e.matmul(
                            ps[5][:, 0:4], ones_full[0:4, 0:128], bct[0:4, 0:4],
                            start=True, stop=True),
                            reads=["bct", "PS5"], writes=["PS5"])
                        P.op("scalar", lambda e: e.copy(nrm[:, 0:4], ps[5][:, 0:4]),
                             reads=["PS5"], writes=["nrm0", "nrm1", "nrm2", "nrm3", "PS5"])
                        P.default_fd = 512
                        for h4 in range(4):
                            h = cp * 4 + h4
                            rs = ts(h4, 32)
                            opid = 6 + (h4 % 2)
                            for j in range(4):
                                nkt = 4 * (j + 1)
                                for kt in range(nkt + 1):
                                    if kt < nkt:
                                        pp = kt % 4
                                        P.op("tensor", lambda e, kt=kt, j=j, rs=rs, pp=pp: e.matmul(
                                            ps[pp][:],
                                            qk[rs, 2 * T + kt * 128: 2 * T + (kt + 1) * 128],
                                            qk[rs, j * 512:(j + 1) * 512],
                                            start=True, stop=False,
                                            tile_position=(rs.start, 0)),
                                            reads=["qk0", "qk2", "PS%d" % pp],
                                            writes=["PS%d" % pp])
                                        P.op("tensor", lambda e, kt=kt, j=j, rs=rs, pp=pp: e.matmul(
                                            ps[pp][:],
                                            qk[rs, 3 * T + kt * 128: 3 * T + (kt + 1) * 128],
                                            qk[rs, T + j * 512: T + (j + 1) * 512],
                                            start=False, stop=True,
                                            tile_position=(rs.start, 0)),
                                            reads=["qk1", "qk3", "PS%d" % pp],
                                            writes=["PS%d" % pp])
                                        P.op("scalar", lambda e, kt=kt, h4=h4, pp=pp: e.activation(
                                            ptil[:, (kt % 4) * 512:(kt % 4 + 1) * 512],
                                            ps[pp][:], AF.Exp, scale=0.125,
                                            bias=nrm[:, h4:h4 + 1]),
                                            reads=["PS%d" % pp, "nrm%d" % h4],
                                            writes=["pt%d" % (kt % 4), "PS%d" % pp])
                                        off = kt - 4 * j
                                        if off >= 0:
                                            P.op("vector", lambda e, kt=kt, off=off: e.tensor_tensor(
                                                ptil[:, (kt % 4) * 512:(kt % 4 + 1) * 512],
                                                ptil[:, (kt % 4) * 512:(kt % 4 + 1) * 512],
                                                masks[:, 384 - 128 * off: 896 - 128 * off], ALU.mult),
                                                reads=["pt%d" % (kt % 4)],
                                                writes=["pt%d" % (kt % 4)])
                                    if kt > 0:
                                        ktp = kt - 1
                                        P.op("tensor", lambda e, ktp=ktp, h=h, opid=opid, nkt=nkt: e.matmul(
                                            ps[opid][0:65, :],
                                            vstore[:, ktp * 1040 + h * 65: ktp * 1040 + (h + 1) * 65],
                                            ptil[:, (ktp % 4) * 512:(ktp % 4 + 1) * 512],
                                            start=(ktp == 0), stop=(ktp == nkt - 1)),
                                            reads=["pt%d" % (ktp % 4), "vst%d" % ktp,
                                                   "PS%d" % opid],
                                            writes=["PS%d" % opid])
                                jp = j % 2
                                oscs = osc[:, 0:512]
                                P.op("scalar", lambda e, opid=opid: e.copy(
                                    denst[64:65, 0:512], ps[opid][64:65, :]),
                                    reads=["PS%d" % opid], writes=["dentmp"])
                                dma(denst[h:h + 1, j * 512:(j + 1) * 512],
                                    denst[64:65, 0:512],
                                    reads=["dentmp"], writes=["den%d" % h])
                                ch = h // 2
                                if h % 2 == 0:
                                    P.op("scalar", lambda e, opid=opid, ch=ch, j=j: e.copy(
                                        otq[0:64, ch * T + j * 512: ch * T + (j + 1) * 512],
                                        ps[opid][0:64, :]),
                                        reads=["PS%d" % opid],
                                        writes=["ot%d" % ch, "PS%d" % opid])
                                else:
                                    P.op("scalar", lambda e, opid=opid, oscs=oscs: e.copy(
                                        oscs[0:64, :], ps[opid][0:64, :]),
                                        reads=["PS%d" % opid],
                                        writes=["oscd%d" % jp, "PS%d" % opid])
                                    dma(otq[64:128, ch * T + j * 512: ch * T + (j + 1) * 512],
                                        oscs[0:64, :], reads=["oscd%d" % jp],
                                        writes=["ot%d" % ch])
                    dtags = ["den%d" % hh for hh in range(16)]
                    for _w in range(8):
                        P.op("tensor", lambda e: e.matmul(
                            ps[5][0:1, 4:512], ones_full[0:1, 0:1],
                            masks[0:1, 0:508], start=True, stop=True),
                            reads=[], writes=["WARM"])
                    P.op("vector", lambda e: e.reciprocal(denst[0:16, :], denst[0:16, :]),
                         reads=dtags, writes=["recd"])
                    for ch in range(NCH):
                        for qt in range(4):
                            P.op("tensor", lambda e, ch=ch, qt=qt: e.matmul(
                                ps[qt][:], selH[0:16, ch * 128:(ch + 1) * 128],
                                denst[0:16, qt * 512:(qt + 1) * 512],
                                start=True, stop=True),
                                reads=["recd", "selH", "PS%d" % qt], writes=["PS%d" % qt])
                            P.op("vector", lambda e, ch=ch, qt=qt: e.tensor_copy(
                                rrb[:, qt * 512:(qt + 1) * 512], ps[qt][:]),
                                reads=["PS%d" % qt], writes=["rrb", "PS%d" % qt])
                        P.op("vector", lambda e, ch=ch: e.tensor_tensor(
                            otq[:, ch * T:(ch + 1) * T], otq[:, ch * T:(ch + 1) * T],
                            rrb[:], ALU.mult),
                            reads=["ot%d" % ch, "rrb"], writes=["ot%d" % ch])
                    if b == 0:
                        for cdump in range(NCH):
                            dump("H", 12 + cdump, otq.ap()[:, cdump * T:(cdump + 1) * T],
                                 ["ot%d" % cdump])
                    load_w("wvbA", 3, wvb.ap()[:, 0:1024], 0, 128,
                           extra=["wvbA"])
                    for et in range(NCH):
                        if et + 1 < NCH:
                            par = (et + 1) % 2
                            load_w("wvb" + "AB"[par], 3,
                                   wvb.ap()[:, par * 1024:(par + 1) * 1024],
                                   (et + 1) * 128, 128,
                                   extra=(["wvbA"] if et + 1 == 1 else ()))
                        ho = (et % 2) * 1024
                        for j in range(4):
                            pid = 4 + j
                            for c in range(NCH):
                                P.op("tensor", lambda e, et=et, j=j, c=c, pid=pid, ho=ho: e.matmul(
                                    ps[pid][:],
                                    wvb[:, ho + c * 128: ho + (c + 1) * 128],
                                    otq[:, c * T + j * 512: c * T + (j + 1) * 512],
                                    start=(c == 0), stop=(c == NCH - 1)),
                                    reads=["wvb" + "AB"[et % 2], "ot%d" % c, "PS%d" % pid],
                                    writes=["PS%d" % pid])
                            fq_in = "fb:%d" % (4 + j)
                            fq_out = "fb:%d" % j
                            dma(fb[:, T + j * 512: T + (j + 1) * 512],
                                xin[b, ts(et), j * 512:(j + 1) * 512], writes=[fq_in])
                            P.op("vector", lambda e, j=j, et=et, pid=pid: e.scalar_tensor_tensor(
                                fb[:, j * 512:(j + 1) * 512], ps[pid][:],
                                biasoa[:, et:et + 1],
                                fb[:, T + j * 512: T + (j + 1) * 512],
                                ALU.add, ALU.add),
                                reads=["PS%d" % pid, fq_in], writes=[fq_out, "PS%d" % pid])
                            if probe and b == 0 and et == 0 and j == 0:
                                dump("F", 4, fb[:, 0:512], [fq_out], w=512)
                                dump("F", 5, fb[:, T:T + 512], [fq_in], w=512)
                            dma(xout[b, ts(et), j * 512:(j + 1) * 512],
                                fb[:, j * 512:(j + 1) * 512],
                                reads=[fq_out], writes=["XOUT"])
                    P.barrier()
                P.default_fd = 0
            P.barrier()

        ffn_uid = [0]

        def ffn(xin, xout):
            fes = contextlib.ExitStack()
            uid = ffn_uid[0]
            ffn_uid[0] += 1
            with fes:
                hbuf = fes.enter_context(nc.sbuf_tensor("hbuf%d" % uid, [128, NCH * T], F32))
                yb = fes.enter_context(nc.sbuf_tensor("yb%d" % uid, [128, NCH * T], BF16))
                wstr = fes.enter_context(nc.sbuf_tensor("wstr%d" % uid, [128, 4 * 1024], BF16))
                lbcur = fes.enter_context(nc.sbuf_tensor("lbcur%d" % uid, [32, 1024], BF16))
                stfb = fes.enter_context(nc.sbuf_tensor("stfb%d" % uid, [4, 1024], BF16))
                murep = fes.enter_context(nc.sbuf_tensor("murep%d" % uid, [128, 512], BF16))
                rsrep = fes.enter_context(nc.sbuf_tensor("rsrep%d" % uid, [128, 512], BF16))
                ubuf = fes.enter_context(nc.sbuf_tensor("ubuf%d" % uid, [32, T], BF16))
                stf = scr
                P.default_fd = 512

                def linear(idx, blk, ln_after):
                    wsrc = wstash[idx].rearrange("(c p) f -> p c f", p=128)
                    dma(statf[0:32, 0:1024], lbT[idx], writes=["statf"])
                    P.op("vector", lambda e: e.tensor_copy(lbcur[:], statf[0:32, 0:1024]),
                         reads=["statf"], writes=["FLB"])
                    for qt in range(4):
                        pid = 6 + qt % 2
                        for c in range(NCH):
                            P.op("tensor", lambda e, qt=qt, c=c, pid=pid, idx=idx: e.matmul(
                                ps[pid][0:32, :],
                                lastore[:, idx * 256 + c * 32: idx * 256 + (c + 1) * 32],
                                xhat[:, c * T + qt * 512: c * T + (qt + 1) * 512],
                                start=(c == 0), stop=(c == NCH - 1)),
                                reads=["xh%d" % c, "PS%d" % pid], writes=["PS%d" % pid])
                        P.op("scalar", lambda e, qt=qt, pid=pid: e.copy(
                            ubuf[0:32, qt * 512:(qt + 1) * 512], ps[pid][0:32, :]),
                            reads=["PS%d" % pid], writes=["FU", "PS%d" % pid])
                    for ot in range(NCH):
                        par = ot % 4
                        dma(wstr.ap()[:, par * 1024:(par + 1) * 1024]
                            .rearrange("p (c f) -> p c f", c=NCH),
                            wsrc[:, :, ot * 128:(ot + 1) * 128],
                            writes=["FW%d" % par])
                        for c in range(NCH):
                            for qt in range(4):
                                P.op("tensor", lambda e, c=c, qt=qt, par=par: e.matmul(
                                    ps[qt][:],
                                    wstr[:, par * 1024 + c * 128: par * 1024 + (c + 1) * 128],
                                    xhat[:, c * T + qt * 512: c * T + (qt + 1) * 512],
                                    start=(c == 0), stop=False),
                                    reads=["FW%d" % par, "xh%d" % c, "PS%d" % qt],
                                    writes=["PS%d" % qt])
                        for qt in range(4):
                            P.op("tensor", lambda e, ot=ot, qt=qt, idx=idx: e.matmul(
                                ps[qt][:],
                                lbcur[0:32, ot * 128:(ot + 1) * 128],
                                ubuf[0:32, qt * 512:(qt + 1) * 512],
                                start=False, stop=True),
                                reads=["FU", "FLB", "PS%d" % qt], writes=["PS%d" % qt])
                        bcol = fbc[:, idx * 8 + ot: idx * 8 + ot + 1]
                        if ln_after:
                            for qt in range(4):
                                P.op("scalar", lambda e, ot=ot, qt=qt, bcol=bcol: e.activation(
                                    yb[:, ot * T + qt * 512: ot * T + (qt + 1) * 512],
                                    ps[qt][:], AF.Identity, bias=bcol),
                                    reads=["PS%d" % qt], writes=["FY%d" % ot])
                                P.op("scalar", lambda e, ot=ot, qt=qt, bcol=bcol: e.activation(
                                    sqb[:, qt * 512:(qt + 1) * 512],
                                    ps[qt][:], AF.Square, bias=bcol),
                                    reads=["PS%d" % qt], writes=["sqb", "PS%d" % qt])
                            for qt in range(4):
                                P.op("tensor", lambda e, ot=ot, qt=qt: e.matmul(
                                    ps[4][0:4, :], stat4[:, 4 * qt: 4 * qt + 4],
                                    yb[:, ot * T + qt * 512: ot * T + (qt + 1) * 512],
                                    start=(ot == 0 and qt == 0), stop=(ot == NCH - 1)),
                                    reads=["FY%d" % ot, "PS4"], writes=["PS4"])
                                P.op("tensor", lambda e, ot=ot, qt=qt: e.matmul(
                                    ps[5][0:4, :], stat4[:, 4 * qt: 4 * qt + 4],
                                    sqb[:, qt * 512:(qt + 1) * 512],
                                    start=(ot == 0 and qt == 0), stop=(ot == NCH - 1)),
                                    reads=["sqb", "PS5"], writes=["PS5"])
                        else:
                            for qt in range(4):
                                P.op("vector", lambda e, ot=ot, qt=qt, bcol=bcol: e.scalar_tensor_tensor(
                                    hbuf[:, ot * T + qt * 512: ot * T + (qt + 1) * 512],
                                    ps[qt][:], bcol,
                                    hbuf[:, ot * T + qt * 512: ot * T + (qt + 1) * 512],
                                    ALU.add, ALU.add),
                                    reads=["PS%d" % qt, "FH%d" % ot],
                                    writes=["FH%d" % ot, "PS%d" % qt])
                    if not ln_after:
                        return
                    P.op("scalar", lambda e: e.copy(stf[0:4, 0:512], ps[4][0:4, :]),
                         reads=["PS4"], writes=["scr", "PS4"])
                    P.op("vector", lambda e: e.tensor_tensor(
                        stf[0:4, 512:1024], stf[0:4, 0:512], stf[0:4, 0:512], ALU.mult),
                        reads=["scr"], writes=["scr"])
                    P.op("vector", lambda e: e.tensor_tensor(
                        stf[0:4, 512:1024], ps[5][0:4, :], stf[0:4, 512:1024],
                        ALU.subtract),
                        reads=["scr", "PS5"], writes=["scr", "PS5"])
                    P.op("scalar", lambda e: e.activation(
                        stf[0:4, 512:1024], stf[0:4, 512:1024], AF.Ln,
                        bias=epsc[0:4, 0:1]),
                        reads=["scr"], writes=["scr"])
                    P.op("scalar", lambda e: e.activation(
                        stf[0:4, 1024:1536], stf[0:4, 512:1024], AF.Exp, scale=-0.5),
                        reads=["scr"], writes=["scr"])
                    P.op("vector", lambda e: e.tensor_copy(stfb[0:4, 0:512], stf[0:4, 0:512]),
                         reads=["scr"], writes=["FSB"])
                    P.op("vector", lambda e: e.tensor_copy(stfb[0:4, 512:1024],
                                                           stf[0:4, 1024:1536]),
                         reads=["scr"], writes=["FSB"])
                    for qt in range(4):
                        P.op("tensor", lambda e, qt=qt: e.matmul(
                            ps[6][:], sel4[0:4, qt * 128:(qt + 1) * 128],
                            stfb[0:4, 0:512], start=True, stop=True),
                            reads=["FSB", "PS6"], writes=["PS6"])
                        P.op("vector", lambda e: e.tensor_copy(murep[:], ps[6][:]),
                             reads=["PS6"], writes=["MR", "PS6"])
                        P.op("tensor", lambda e, qt=qt: e.matmul(
                            ps[6][:], sel4[0:4, qt * 128:(qt + 1) * 128],
                            stfb[0:4, 512:1024], start=True, stop=True),
                            reads=["FSB", "PS6"], writes=["PS6"])
                        P.op("vector", lambda e: e.tensor_copy(rsrep[:], ps[6][:]),
                             reads=["PS6"], writes=["RR2", "PS6"])
                        for ot in range(NCH):
                            sl = slice(ot * T + qt * 512, ot * T + (qt + 1) * 512)
                            P.op("vector", lambda e, sl=sl: e.tensor_tensor(
                                yb[:, sl], yb[:, sl], murep[:], ALU.subtract),
                                reads=["FY%d" % ot, "MR"], writes=["FY%d" % ot])
                            P.op("vector", lambda e, sl=sl: e.tensor_tensor(
                                yb[:, sl], yb[:, sl], rsrep[:], ALU.mult),
                                reads=["FY%d" % ot, "RR2"], writes=["FY%d" % ot])
                            P.op("scalar", lambda e, sl=sl, ot=ot, blk=blk: e.activation(
                                xhat[:, sl], yb[:, sl], AF.Gelu,
                                bias=btc[:, blk * 8 + ot: blk * 8 + ot + 1],
                                scale=gmc[:, blk * 8 + ot: blk * 8 + ot + 1]),
                                reads=["FY%d" % ot], writes=["xh%d" % ot])

                for b in range(NB):
                    if b == 0 and probe:
                        dump("F", 8, xin[0, ts(0), :], ["XOUT"])
                    ln_phase(xin, b)
                    if b == 0 and probe:
                        dump("H", 12, xhat[:, 0:T], ["xh0"])
                    for c in range(NCH):
                        P.op("vector", lambda e, c=c: e.tensor_copy(
                            hbuf[:, ts(c, T)], xhat[:, ts(c, T)]),
                            reads=["xh%d" % c], writes=["FH%d" % c])
                    for k in range(3):
                        linear(2 * k, k, True)
                        if b == 0 and k == 0 and probe:
                            dump("H", 13, yb.ap()[:, 0:T], ["FY0"])
                            dump("F", 7, stf[0:4, 0:1536], ["scr"], w=1536, p=4)
                            dump("H", 14, xhat[:, 0:T], ["xh0"])
                            dump("H", 15, ubuf[0:32, 0:T], ["FU"], p=32)
                        linear(2 * k + 1, k, False)
                        if b == 0 and k == 0 and probe:
                            dump("F", 6, hbuf.ap()[:, 0:T], ["FH0"])
                        if k < 2:
                            for c in range(NCH):
                                P.op("scalar", lambda e, c=c: e.copy(
                                    xhat[:, ts(c, T)], hbuf[:, ts(c, T)]),
                                    reads=["FH%d" % c], writes=["xh%d" % c])
                    for et in range(NCH):
                        for j in range(4):
                            fq_in = "fb:%d" % (4 + j)
                            fq_out = "fb:%d" % j
                            dma(fb[:, T + j * 512: T + (j + 1) * 512],
                                xin[b, ts(et), j * 512:(j + 1) * 512], writes=[fq_in])
                            P.op("vector", lambda e, et=et, j=j: e.tensor_tensor(
                                fb[:, j * 512:(j + 1) * 512],
                                hbuf[:, et * T + j * 512: et * T + (j + 1) * 512],
                                fb[:, T + j * 512: T + (j + 1) * 512], ALU.add),
                                reads=["FH%d" % et, fq_in], writes=[fq_out])
                            dma(xout[b, ts(et), j * 512:(j + 1) * 512],
                                fb[:, j * 512:(j + 1) * 512],
                                reads=[fq_out], writes=["XOUT"])
                    P.barrier()
                P.default_fd = 0
            P.barrier()

        def scan(xin, xout):
            ses = contextlib.ExitStack()
            with ses:
                wdb = ses.enter_context(nc.sbuf_tensor("wdb", [128, NCH * 128], BF16))
                wob2 = ses.enter_context(nc.sbuf_tensor("wob2", [128, NCH * 128], BF16))
                na = ses.enter_context(nc.sbuf_tensor("na", [128, 128], F32))
                bp = ses.enter_context(nc.sbuf_tensor("bp", [128, 128], F32))
                df = ses.enter_context(nc.sbuf_tensor("df", [128, T], F32))
                uf = ses.enter_context(nc.sbuf_tensor("uf", [128, T], F32))
                Af = ses.enter_context(nc.sbuf_tensor("Af", [128, 4 * T], F32))
                Bf = ses.enter_context(nc.sbuf_tensor("Bf", [128, 4 * T], BF16))
                yac = ses.enter_context(nc.sbuf_tensor("yac", [128, T], BF16))
                stt_ = ses.enter_context(nc.sbuf_tensor("stt_", [128, 2 * T], BF16))
                yT = ses.enter_context(nc.sbuf_tensor("yT", [128, NCH * T], BF16))

                for c in range(NCH):
                    dma(na[:, c * 16:(c + 1) * 16], alog[ts(c), :], writes=["na"])
                    dma(bp[:, c * 16:(c + 1) * 16], bpar[ts(c), :], writes=["bp"])
                P.op("scalar", lambda e: e.activation(na[:], na[:], AF.Exp),
                     reads=["na"], writes=["na"])
                P.op("scalar", lambda e: e.activation(na[:], na[:], AF.Ln, bias=1.0),
                     reads=["na"], writes=["na"])
                P.op("vector", lambda e: e.tensor_scalar(na[:], na[:], -1.0, -1e-4,
                                                         ALU.mult, ALU.add),
                     reads=["na"], writes=["na"])
                P.default_fd = 512

                def sload(dst, src_ap, et, tag):
                    for c in range(NCH):
                        wfq = "fb:%d" % (4 + c % 2)
                        dma(fb[:, T + (c % 2) * 512: T + (c % 2) * 512 + 128],
                            src_ap[ts(c), ts(et)], writes=[wfq])
                        P.op("scalar", lambda e, c=c: e.copy(
                            dst[:, c * 128:(c + 1) * 128],
                            fb[:, T + (c % 2) * 512: T + (c % 2) * 512 + 128]),
                            reads=[wfq], writes=[tag])
                for b in range(NB):
                    ln_phase(xin, b)
                    for et in range(NCH):
                        sload(wdb, wdT, et, "WD")
                        for c in range(NCH):
                            for qt in range(4):
                                P.op("tensor", lambda e, et=et, c=c, qt=qt: e.matmul(
                                    ps[qt][:],
                                    wdb[:, c * 128:(c + 1) * 128],
                                    xhat[:, c * T + qt * 512: c * T + (qt + 1) * 512],
                                    start=(c == 0), stop=(c == NCH - 1)),
                                    reads=["WD", "xh%d" % c, "PS%d" % qt],
                                    writes=["PS%d" % qt])
                        for qt in range(4):
                            P.op("scalar", lambda e, et=et, qt=qt: e.activation(
                                df[:, qt * 512:(qt + 1) * 512], ps[qt][:], AF.Exp,
                                bias=bdcol[:, et:et + 1]),
                                reads=["PS%d" % qt], writes=["SD", "PS%d" % qt])
                            P.op("scalar", lambda e, qt=qt: e.activation(
                                df[:, qt * 512:(qt + 1) * 512],
                                df[:, qt * 512:(qt + 1) * 512], AF.Ln, bias=1.0),
                                reads=["SD"], writes=["SD"])
                        P.op("vector", lambda e, et=et: e.tensor_tensor(
                            uf[:], df[:], xhat[:, ts(et, T)], ALU.mult),
                            reads=["SD", "xh%d" % et], writes=["SU"])
                        for s in range(16):
                            pa = s % 4
                            paA = s % 4
                            A = Af[:, paA * T:(paA + 1) * T]
                            Bv = Bf[:, pa * T:(pa + 1) * T]
                            nac = na[:, et * 16 + s: et * 16 + s + 1]
                            bpc = bp[:, et * 16 + s: et * 16 + s + 1]
                            P.op("scalar", lambda e, nac=nac, A=A: e.activation(
                                A, df[:], AF.Exp, scale=nac),
                                reads=["SD", "na"], writes=["SA%d" % paA])
                            P.op("scalar", lambda e, bpc=bpc, Bv=Bv: e.activation(
                                Bv, uf[:], AF.Identity, scale=bpc),
                                reads=["SU", "bp"], writes=["SB%d" % pa])
                            if s == 0:
                                P.op("vector", lambda e, A=A, Bv=Bv: e.tensor_tensor_scan(
                                    yac[:], A, Bv, 0.0, ALU.mult, ALU.add),
                                    reads=["SA%d" % paA, "SB%d" % pa], writes=["SY"])
                            else:
                                sp2 = s % 2
                                st2 = stt_[:, sp2 * T:(sp2 + 1) * T]
                                P.op("vector", lambda e, A=A, Bv=Bv, st2=st2: e.tensor_tensor_scan(
                                    st2, A, Bv, 0.0, ALU.mult, ALU.add),
                                    reads=["SA%d" % paA, "SB%d" % pa], writes=["SS%d" % sp2])
                                P.op("vector", lambda e, st2=st2: e.tensor_tensor(
                                    yac[:], yac[:], st2, ALU.add),
                                    reads=["SY", "SS%d" % sp2], writes=["SY"])
                        P.op("vector", lambda e, et=et: e.tensor_copy(
                            yT[:, ts(et, T)], yac[:]),
                            reads=["SY"], writes=["yT%d" % et])
                    for et in range(NCH):
                        sload(wob2, wobT, et, "WB")
                        for j in range(4):
                            pid = 4 + j
                            for c in range(NCH):
                                P.op("tensor", lambda e, et=et, j=j, c=c, pid=pid: e.matmul(
                                    ps[pid][:],
                                    wob2[:, c * 128:(c + 1) * 128],
                                    yT[:, c * T + j * 512: c * T + (j + 1) * 512],
                                    start=(c == 0), stop=(c == NCH - 1)),
                                    reads=["WB", "yT%d" % c, "PS%d" % pid],
                                    writes=["PS%d" % pid])
                            fq_in = "fb:%d" % (4 + j)
                            fq_out = "fb:%d" % j
                            dma(fb[:, T + j * 512: T + (j + 1) * 512],
                                xin[b, ts(et), j * 512:(j + 1) * 512], writes=[fq_in])
                            P.op("vector", lambda e, et=et, j=j, pid=pid: e.scalar_tensor_tensor(
                                fb[:, j * 512:(j + 1) * 512], ps[pid][:],
                                bobcol[:, et:et + 1],
                                fb[:, T + j * 512: T + (j + 1) * 512],
                                ALU.add, ALU.add),
                                reads=["PS%d" % pid, fq_in], writes=[fq_out, "PS%d" % pid])
                            dma(xout[b, ts(et), j * 512:(j + 1) * 512],
                                fb[:, j * 512:(j + 1) * 512],
                                reads=[fq_out], writes=["XOUT"])
                    P.barrier()
                P.default_fd = 0
            P.barrier()

        cur = xT
        if n_sub >= 1:
            attn_a(cur, Xs[0])
            cur = Xs[0]
        if n_sub >= 2:
            ffn(cur, Xs[1])
            cur = Xs[1]
        if n_sub >= 3:
            scan(cur, Xs[0])
            cur = Xs[0]
        if n_sub >= 4:
            ffn(cur, out)
            cur = None
        def copy_hbm(dst_ap, src_ap, tagbase):
            for b in range(NB):
                for c in range(NCH):
                    fq = ["fb:%d" % q for q in range((c % 2) * 4, (c % 2) * 4 + 4)]
                    dma(fb[:, ts(c % 2, T)], src_ap[b, ts(c), :], writes=fq)
                    dma(dst_ap[b, ts(c), :], fb[:, ts(c % 2, T)],
                        reads=fq, writes=[tagbase])

        for s, ap in dbg.items():
            copy_hbm(ap, Xs[0] if s == 1 else cur, "DBG")
        if cur is not None:
            copy_hbm(out, cur, "OUT")
        P.op("vector", lambda e: e.engine_nop(), reads=["OUT", "DBG", "PRB", "XOUT"])

        with nc.allow_low_precision(reason="deliberate bf16 rounding at bf16-matmul noise level"):
            P.materialize(block, sems, dmapool)
    return nc


def _perm_qk():
    p = np.zeros(1024, dtype=np.int64)
    for r in range(512):
        p[r] = (r // 32) * 64 + (r % 32)
        p[512 + r] = (r // 32) * 64 + 32 + (r % 32)
    return p


def _cols(v):
    return np.ascontiguousarray(v.reshape(8, 128).T)


N_SUB = 4


def host_inputs(inp):
    perm = _perm_qk()
    Wq = np.asarray(inp["L0_attn_a_Wq"], np.float32)[:, perm]
    Wkv = np.asarray(inp["L0_attn_a_Wkv"], np.float32)
    Wk = Wkv[:, :1024][:, perm]
    Wv = Wkv[:, 1024:]
    bq = np.asarray(inp["L0_attn_a_bq"], np.float32)[perm]
    bkv = np.asarray(inp["L0_attn_a_bkv"], np.float32)
    bk = bkv[:1024][perm]
    bv = bkv[1024:]
    Woa = np.asarray(inp["L0_attn_a_Wo"], np.float32)
    boa = np.asarray(inp["L0_attn_a_bo"], np.float32)
    qw = np.asarray(inp["L0_ffn_qweights"], np.int32)
    sc = np.asarray(inp["L0_ffn_scales"], np.float32)
    screp = np.ascontiguousarray(np.repeat(sc, 16, axis=2).transpose(0, 2, 1))
    la = np.asarray(inp["L0_ffn_lora_a"], np.float32)
    lb = np.asarray(inp["L0_ffn_lora_b"], np.float32)
    fb_ = np.asarray(inp["L0_ffn_biases"], np.float32)
    gam = np.asarray(inp["L0_ffn_ln_gamma"], np.float32)
    bet = np.asarray(inp["L0_ffn_ln_beta"], np.float32)
    com = {
        "wq": np.ascontiguousarray(Wq), "wk": np.ascontiguousarray(Wk),
        "wv": np.ascontiguousarray(Wv), "woa": np.ascontiguousarray(Woa),
        "bqc": _cols(bq), "bkc": _cols(bk),
        "bvr": np.ascontiguousarray(bv[None, :]), "boac": _cols(boa),
        "qT": np.ascontiguousarray(qw.transpose(0, 2, 1)),
        "screp": screp,
        "laT": np.ascontiguousarray(la.transpose(0, 2, 1)),
        "lbT": np.ascontiguousarray(lb.transpose(0, 2, 1)),
        "fbias": np.concatenate([_cols(fb_[l]) for l in range(6)], 1),
        "gamc": np.concatenate([_cols(gam[l]) for l in range(3)], 1),
        "betc": np.concatenate([_cols(bet[l]) for l in range(3)], 1),
        "wdT": np.ascontiguousarray(np.asarray(inp["L0_attn_b_Wd"], np.float32).T),
        "wobT": np.ascontiguousarray(np.asarray(inp["L0_attn_b_Wo"], np.float32).T),
        "bdc": _cols(np.asarray(inp["L0_attn_b_bd"], np.float32)),
        "bobc": _cols(np.asarray(inp["L0_attn_b_bo"], np.float32)),
        "alog": np.ascontiguousarray(np.asarray(inp["L0_attn_b_a_log"], np.float32)),
        "bpar": np.ascontiguousarray(np.asarray(inp["L0_attn_b_b_param"], np.float32)),
    }
    return com


def kernel(**inp):
    x = np.asarray(inp["L0_attn_a_x"], dtype=np.float32)
    com = host_inputs(inp)
    nc = build_nc(n_sub=N_SUB)
    in_maps = []
    for c in range(8):
        m = dict(com)
        m["xT"] = np.ascontiguousarray(x[2 * c: 2 * c + 2].transpose(0, 2, 1))
        in_maps.append(m)
    res = run_bass_kernel_spmd(nc, in_maps, core_ids=list(range(8)))
    outs = [np.asarray(res.results[c]["outT"]).transpose(0, 2, 1)
            for c in range(8)]
    return np.ascontiguousarray(np.concatenate(outs, axis=0))
